# version 09
# speedup vs baseline: 1.3587x; 1.0282x over previous
import math
from contextlib import ExitStack

import numpy as np
import ml_dtypes

import concourse.bass as bass
import concourse.mybir as mybir
from concourse.bass_utils import run_bass_kernel_spmd

F32 = mybir.dt.float32
BF16 = mybir.dt.bfloat16
AF = mybir.ActivationFunctionType
ALU = mybir.AluOpType
AX = mybir.AxisListType
BF = ml_dtypes.bfloat16

T = 4096
D = 1024
NCH = 8
TS = 512
HD = 64

ENG = ("tensor", "vector", "scalar", "sync", "gpsimd")
CANON = {"tensor": "pe", "vector": "dve", "scalar": "act", "sync": "sq", "gpsimd": "wl"}
NPOOL = 40


class G:
    __slots__ = ("w", "rs")

    def __init__(self):
        self.w = None
        self.rs = []


def garr(*dims):
    if len(dims) == 1:
        return [G() for _ in range(dims[0])]
    return [garr(*dims[1:]) for _ in range(dims[0])]


class K:
    def __init__(self, nc, sim_safe=False):
        self.sim_safe = sim_safe
        self.nc = nc
        self.recs = {e: [] for e in ENG}
        self.pend = {e: [] for e in ENG}
        self.dram = {}
        self.sems = {}
        self.dma_count = {e: 0 for e in ENG}
        self.dma_last = {e: {} for e in ENG}

    def op(self, eng, mk, reads=(), writes=(), rw=(), dma=False, hbm_only=False):
        def flat(gs):
            out = []
            for g in gs:
                if isinstance(g, G):
                    out.append(g)
                else:
                    out.extend(flat(g))
            return out
        rd, wr, rwg = flat(reads), flat(writes), flat(rw)
        waits = set()
        for g in rd + rwg:
            if g.w is not None:
                waits.add(g.w)
        for g in wr + rwg:
            if g.w is not None:
                waits.add(g.w)
            for t in g.rs:
                waits.add(t)
        tok = (eng, len(self.recs[eng]))
        safe = self.sim_safe is True or (self.sim_safe and eng in self.sim_safe)
        if not safe:
            waits = {t for t in waits if t[0] != eng}
            waits.update(t for t in self.pend[eng] if t[0] != eng)
        else:
            waits.update(self.pend[eng])
        self.pend[eng] = []
        self.recs[eng].append({"mk": mk, "waits": waits, "dma": dma,
                               "scope": getattr(self, "scope", "init")})
        if dma:
            pi = self.dma_count[eng] % NPOOL
            self.dma_count[eng] += 1
            if not hbm_only:
                self.dma_last[eng][pi] = tok
        for g in wr + rwg:
            g.w = tok
            g.rs = []
        for g in rd:
            g.rs.append(tok)
        return tok

    def fence(self, engines=("tensor", "vector", "scalar", "sync")):
        toks = [(e, len(self.recs[e]) - 1) for e in engines if self.recs[e]]
        for de in ("sync", "gpsimd"):
            toks.extend(self.dma_last[de].values())
        for e in engines:
            self.pend[e].extend(toks)

    def finalize(self, block):
        needed = set()
        for e in ENG:
            for r in self.recs[e]:
                needed.update(r["waits"])
        toksem = {}
        for e in ENG:
            c = 0
            dma_i = 0
            pool_use = {}
            for i, r in enumerate(self.recs[e]):
                if r["dma"]:
                    pn = "%s%d" % (CANON[e], dma_i % NPOOL)
                    dma_i += 1
                    u = pool_use.get(pn, 0) + 1
                    pool_use[pn] = u
                    r["inc_sem"] = pn
                    r["inc"] = 16
                    r["selfwait"] = (pn, 16 * (u - 1)) if u > 1 else None
                    toksem[(e, i)] = (pn, 16 * u)
                elif (e, i) in needed:
                    c += 1
                    r["inc_sem"] = CANON[e]
                    r["inc"] = 1
                    r["selfwait"] = None
                    toksem[(e, i)] = (CANON[e], c)
                else:
                    r["inc"] = 0
                    r["selfwait"] = None
        self.n_inst = sum(len(self.recs[e]) for e in ENG)
        finals = {}
        for e in ENG:
            for i, r in enumerate(self.recs[e]):
                if r["inc"] and r["dma"]:
                    nm, v = toksem[(e, i)]
                    finals[nm] = max(finals.get(nm, 0), v)
        for e in ENG:
            recs = self.recs[e]

            def gen(eng, recs=recs, e=e):
                cur = [None, None]
                for r in recs:
                    if r["scope"] != cur[0]:
                        if cur[0] is not None:
                            self.nc.leave_named_scope(cur[0], cur[1], False)
                        sid, _ = self.nc.enter_named_scope(r["scope"], False)
                        cur[0], cur[1] = r["scope"], sid
                    byname = {}
                    for t in r["waits"]:
                        nm, v = toksem[t]
                        if byname.get(nm, 0) < v:
                            byname[nm] = v
                    if r["selfwait"] is not None and r["selfwait"][1] > 0:
                        nm, v = r["selfwait"]
                        if byname.get(nm, 0) < v:
                            byname[nm] = v
                    for nm, v in byname.items():
                        eng.wait_ge(self.sems[nm], v)
                    ins = r["mk"](eng)
                    if r["inc"]:
                        ins.then_inc(self.sems[r["inc_sem"]], r["inc"])
                if e == "sync":
                    for nm, v in sorted(finals.items()):
                        eng.wait_ge(self.sems[nm], v)
                if cur[0] is not None:
                    self.nc.leave_named_scope(cur[0], cur[1], False)
            getattr(block, e)(gen)


def prep_pairmajor(w):
    din, dout = w.shape
    a = np.asarray(w).reshape(din // 128, 128, dout // 128, 128).transpose(2, 0, 1, 3)
    return np.ascontiguousarray(a.astype(np.float32))


def prep_col(v):
    return np.ascontiguousarray(np.asarray(v).reshape(-1, 128).T.astype(np.float32))


def prep_moe_w1(w):
    E, din, ff = w.shape
    a = np.asarray(w).reshape(E, din // 128, 128, ff // 128, 128).transpose(0, 3, 1, 2, 4)
    return np.ascontiguousarray(a.astype(np.float32))


def prep_moe_w2(w):
    E, ff, dout = w.shape
    a = np.asarray(w).reshape(E, ff // 128, 128, dout // 128, 128).transpose(0, 3, 1, 2, 4)
    return np.ascontiguousarray(a.astype(np.float32))


def host_consts():
    c = {}
    c["c_identf"] = np.eye(128, dtype=np.float32)
    c["c_ones"] = np.ones((128, 128), dtype=BF)
    c["c_onesf"] = np.ones((1, 128), dtype=np.float32)
    m = np.zeros((128, 4, 512), dtype=BF)
    for o in range(4):
        for kk in range(128):
            m[kk, o, o * 128 + kk:] = 1.0
    c["c_masks"] = m
    s2 = np.zeros((2, 2, 128), dtype=BF)
    s2[0, 0, :] = 1.0
    s2[1, 1, :] = 1.0
    c["c_sel2"] = s2
    s8 = np.zeros((8, 8, 128), dtype=BF)
    for e in range(8):
        s8[e, e, :] = 1.0
    c["c_sel8"] = s8
    ind = np.zeros((128, 2), dtype=BF)
    ind[:64, 0] = 1.0
    ind[64:, 1] = 1.0
    c["c_ind2"] = ind
    b2f = np.zeros((2, 128), dtype=np.float32)
    b2f[0, :64] = 1.0
    b2f[1, 64:] = 1.0
    c["c_b2f"] = b2f
    P2 = np.zeros((128, 128), dtype=np.float32)
    for base in (0, 64):
        for i in range(32):
            P2[base + i + 32, base + i] = 1.0
            P2[base + i, base + i + 32] = 1.0
    c["c_ropeP"] = P2.astype(BF)
    inv = 1.0 / 10000.0 ** (np.arange(0, HD, 2, dtype=np.float64) / HD)
    fr = np.arange(T, dtype=np.float64)[:, None] * inv[None, :]
    emb = np.concatenate([fr, fr], axis=-1)
    cos = np.cos(emb).T.astype(np.float32)
    sin = np.sin(emb).T.astype(np.float32)
    sgn = np.ones((HD, 1), dtype=np.float32)
    sgn[:32] = -1.0
    c["c_cos"] = np.concatenate([cos, cos], axis=0).astype(BF)
    c["c_sin"] = np.concatenate([sin * sgn, sin * sgn], axis=0).astype(BF)
    return c


def prep_inputs(ins):
    p = dict(host_consts())
    p["l0a_wq"] = prep_pairmajor(np.asarray(ins["L0_attn_Wq"]).T)
    p["l0a_wk"] = prep_pairmajor(np.asarray(ins["L0_attn_Wk"]).T)
    p["l0a_wv"] = prep_pairmajor(np.asarray(ins["L0_attn_Wv"]).T)
    p["l0a_wo"] = prep_pairmajor(np.asarray(ins["L0_attn_Wo"]).T)
    p["l0a_bq"] = prep_col(ins["L0_attn_bq"])
    p["l0a_bk"] = prep_col(ins["L0_attn_bk"])
    p["l0a_bo"] = prep_col(ins["L0_attn_bo"])
    p["l0a_bv"] = prep_col(ins["L0_attn_bv"])
    p["l0a_nw"] = prep_col(ins["L0_attn_norm_w"])
    p["l0f_wq"] = prep_pairmajor(ins["L0_ffn_Wq"])
    p["l0f_wk"] = prep_pairmajor(ins["L0_ffn_Wk"])
    p["l0f_wv"] = prep_pairmajor(ins["L0_ffn_Wv"])
    p["l0f_wo"] = prep_pairmajor(ins["L0_ffn_Wo"])
    p["l0f_g"] = prep_col(ins["L0_ffn_ln_g"])
    p["l0f_b"] = prep_col(ins["L0_ffn_ln_b"])
    p["l0f_qs"] = prep_col(np.asarray(ins["L0_ffn_q_scale"]).reshape(D))
    p["l0f_ks"] = prep_col(np.asarray(ins["L0_ffn_k_scale"]).reshape(D))
    p["l0f_gatew"] = np.ascontiguousarray(
        np.asarray(ins["L0_ffn_gate_w"]).reshape(8, 128, 8).astype(np.float32))
    p["l0f_gateb"] = np.asarray(ins["L0_ffn_gate_b"]).reshape(8, 1).astype(np.float32)
    p["l0f_w1"] = prep_moe_w1(np.asarray(ins["L0_ffn_w1"]))
    p["l0f_b1"] = np.ascontiguousarray(
        np.asarray(ins["L0_ffn_b1"]).reshape(8, 32, 128).transpose(2, 0, 1).astype(np.float32))
    p["l0f_w2"] = prep_moe_w2(np.asarray(ins["L0_ffn_w2"]))
    p["l0f_b2"] = np.ascontiguousarray(
        np.asarray(ins["L0_ffn_b2"]).reshape(8, 8, 128).transpose(1, 0, 2).astype(np.float32))
    p["l1a_wq"] = prep_pairmajor(ins["L1_attn_Wq"])
    p["l1a_wk"] = prep_pairmajor(ins["L1_attn_Wk"])
    p["l1a_wv"] = prep_pairmajor(ins["L1_attn_Wv"])
    p["l1a_wo"] = prep_pairmajor(ins["L1_attn_Wpost"])
    p["l1f_wq"] = prep_pairmajor(ins["L1_ffn_wq"])
    p["l1f_wk"] = prep_pairmajor(ins["L1_ffn_wk"])
    p["l1f_wv"] = prep_pairmajor(ins["L1_ffn_wv"])
    p["l1f_wo"] = prep_pairmajor(ins["L1_ffn_wo"])
    p["l1f_g1"] = prep_col(ins["L1_ffn_g1"])
    p["l1f_g2"] = prep_col(ins["L1_ffn_g2"])
    p["l1f_gatew"] = np.ascontiguousarray(
        np.asarray(ins["L1_ffn_gate"]).reshape(8, 128, 4).astype(np.float32))
    p["l1f_w1"] = prep_moe_w1(np.asarray(ins["L1_ffn_w1"]))
    p["l1f_w3"] = prep_moe_w1(np.asarray(ins["L1_ffn_w3"]))
    p["l1f_w2"] = prep_moe_w2(np.asarray(ins["L1_ffn_w2"]))
    return p


WSHAPES = {
    "l0a_wq": (8, 8, 128, 128), "l0a_wk": (8, 8, 128, 128), "l0a_wv": (8, 8, 128, 128),
    "l0a_wo": (8, 8, 128, 128),
    "l0a_bq": (128, 8), "l0a_bk": (128, 8), "l0a_bo": (128, 8), "l0a_bv": (128, 8),
    "l0a_nw": (128, 8),
    "l0f_wq": (8, 8, 128, 128), "l0f_wk": (8, 8, 128, 128), "l0f_wv": (8, 8, 128, 128),
    "l0f_wo": (8, 8, 128, 128),
    "l0f_g": (128, 8), "l0f_b": (128, 8), "l0f_qs": (128, 8), "l0f_ks": (128, 8),
    "l0f_gatew": (8, 128, 8), "l0f_gateb": (8, 1),
    "l0f_w1": (8, 32, 8, 128, 128), "l0f_b1": (128, 8, 32),
    "l0f_w2": (8, 8, 32, 128, 128), "l0f_b2": (8, 8, 128),
    "l1a_wq": (8, 8, 128, 128), "l1a_wk": (8, 8, 128, 128), "l1a_wv": (8, 8, 128, 128),
    "l1a_wo": (8, 8, 128, 128),
    "l1f_wq": (8, 8, 128, 128), "l1f_wk": (2, 8, 128, 128), "l1f_wv": (2, 8, 128, 128),
    "l1f_wo": (8, 8, 128, 128),
    "l1f_g1": (128, 8), "l1f_g2": (128, 8), "l1f_gatew": (8, 128, 4),
    "l1f_w1": (4, 24, 8, 128, 128), "l1f_w3": (4, 24, 8, 128, 128),
    "l1f_w2": (4, 8, 24, 128, 128),
}

CONST_SHAPES = {
    "c_identf": ((128, 128), F32), "c_ones": ((128, 128), BF16),
    "c_onesf": ((1, 128), F32),
    "c_masks": ((128, 4, 512), BF16), "c_sel2": ((2, 2, 128), BF16),
    "c_sel8": ((8, 8, 128), BF16), "c_ind2": ((128, 2), BF16),
    "c_b2f": ((2, 128), F32), "c_ropeP": ((128, 128), BF16),
}
ROPE_SHAPES = {"c_cos": ((128, T), BF16), "c_sin": ((128, T), BF16)}

BLK = {
    "l0a": dict(scale=0.125, causal=True),
    "l0f": dict(scale=10.0, causal=True),
    "l1a": dict(scale=0.125, causal=False),
    "l1f": dict(scale=0.125, causal=True),
}


class Builder:
    def __init__(self, phases=("all",), io_kinds=None, dumps=(), sim_safe=False):
        self.phases = phases
        self.io_kinds = io_kinds or {}
        self.dumps = dumps
        self.sim_safe = sim_safe

    def build(self):
        nc = bass.Bass("TRN2", target_bir_lowering=False, debug=False)
        self.nc = nc
        k = K(nc, sim_safe=self.sim_safe)
        self.k = k
        st = ExitStack()
        self.stk = st

        def dram(name, shape, dtype, kind="Internal"):
            kind = self.io_kinds.get(name, kind)
            h = nc.dram_tensor(name, list(shape), dtype, kind=kind)
            k.dram[name] = h.ap()

        dram("x0", (T, D), F32, kind="ExternalInput")
        dram("outy", (T, D), F32, kind="ExternalOutput")
        if getattr(self, "dumps", ()):
            if "A" in self.dumps:
                dram("dbgA", (NCH, 128, T), BF16, kind="ExternalOutput")
            if "B" in self.dumps:
                dram("dbgB", (NCH, 128, T), BF16, kind="ExternalOutput")
        for nm, (shape, dt) in {**CONST_SHAPES, **ROPE_SHAPES}.items():
            dram(nm, shape, dt, kind="ExternalInput")
        for nm, shape in WSHAPES.items():
            dram(nm, shape, F32, kind="ExternalInput")
        dram("xs", (NCH, 128, T), F32)
        dram("xb", (NCH, 128, T), BF16)
        dram("ub", (NCH, 128, T), BF16)
        dram("moeb", (NCH, 128, T), BF16)
        dram("kvb", (2, 128, T), BF16)
        dram("vtb", (4, 128, 32, 64), BF16)
        dram("w1b_l0", (8, 32, 8, 128, 128), BF16)
        dram("w2b_l0", (8, 8, 32, 128, 128), BF16)
        dram("w1b_l1", (4, 24, 8, 128, 128), BF16)
        dram("w3b_l1", (4, 24, 8, 128, 128), BF16)
        dram("w2b_l1", (4, 8, 24, 128, 128), BF16)

        k.A = st.enter_context(nc.sbuf_tensor("Abuf", [128, 8, 4096], BF16))
        k.Bt = st.enter_context(nc.sbuf_tensor("Bbuf", [128, 32768], BF16))
        k.U = st.enter_context(nc.sbuf_tensor("Ubuf", [128, 20480], BF16))
        k.Wt = st.enter_context(nc.sbuf_tensor("Wbuf", [128, 6144], BF16))
        k.R = st.enter_context(nc.sbuf_tensor("Rows", [128, 3712], F32))
        k.CON = {}
        for nm, (shape, dt) in CONST_SHAPES.items():
            k.CON[nm] = st.enter_context(nc.sbuf_tensor("s" + nm, list(shape), dt))
        k.WV = {}
        k.gWV = {}
        k.PS = st.enter_context(nc.psum_tensor("PSm", [128, 8, 512], F32))
        for s in ("pe", "dve", "act"):
            k.sems[s] = st.enter_context(nc.semaphore(s))
        for base in ("sq", "wl"):
            for i in range(NPOOL):
                nm = "%s%d" % (base, i)
                k.sems[nm] = st.enter_context(nc.semaphore(nm))

        k.gA = garr(8, 8)
        k.gB = garr(64)
        k.gPS = garr(8)
        k.gR = garr(8)
        k.gCON = {nm: G() for nm in CONST_SHAPES}
        k.gXS = garr(8)
        k.gXB = garr(8)
        k.gUB = garr(8)
        k.gMOEB = garr(8)
        k.gKVB = garr(2)
        k.gVTB = garr(4)
        k.gWBF = {"w1b_l0": garr(8), "w2b_l0": garr(8), "w1b_l1": garr(4),
                  "w3b_l1": garr(4), "w2b_l1": garr(4)}
        k.gOUT = garr(8)
        k.gWt = garr(6)
        self._cc = {}
        with nc.Block() as block:
            self.emit_all()
            k.finalize(block)
        st.close()
        self.n_inst = k.n_inst
        return nc

    def Bv(self):
        return self.k.Bt[:].rearrange("p (c t) -> p c t", c=8)

    def Bh(self):
        return self.k.Bt[:].rearrange("p (f t) -> p f t", t=1024)

    def gBr(self, lo, n):
        return self.k.gB[lo // 512:(lo + n + 511) // 512]

    def gBcs(self, dc, ts):
        return self.k.gB[dc * 8 + ts]

    def Ub(self, off, n, parts=128):
        return self.k.U[0:parts, off:off + n]

    def Uf(self, off, n, parts=128):
        return self.k.U[0:parts, off:off + 2 * n].bitcast(F32)

    def wv(self, name):
        k = self.k
        if name not in k.WV:
            shape = list(WSHAPES[name])
            t = self.stk.enter_context(self.nc.sbuf_tensor("wv_" + name, shape, F32))
            g = G()
            k.WV[name] = t
            k.gWV[name] = g
            k.op("sync", lambda e, t=t, name=name: e.dma_start(out=t[:], in_=k.dram[name]),
                 writes=[g], dma=True)
        return k.WV[name], k.gWV[name]

    def ccol(self, v):
        k = self.k
        if v not in self._cc:
            i = len(self._cc)
            assert i < 512
            ap = k.R[:, 3584 + i:3585 + i]
            k.op("vector", lambda e, ap=ap, v=v: e.memset(ap, float(v)), writes=[k.gR[7]])
            self._cc[v] = ap
        return self._cc[v]

    def row(self, slot, n=512):
        assert slot < 7 and n <= 512
        return self.k.R[0:1, slot * 512:slot * 512 + n]

    def row2(self, slot, n=512):
        return self.k.R[0:2, slot * 512:slot * 512 + n]

    def emit_all(self):
        k = self.k
        ph = self.phases
        want = lambda p: ("all" in ph) or (p in ph)
        for nm in CONST_SHAPES:
            k.op("sync", lambda e, nm=nm: e.dma_start(out=k.CON[nm][:], in_=k.dram[nm]),
                 writes=[k.gCON[nm]], dma=True)
        for v in (1.0 / 1024.0, 0.5, -0.5, -1.0, 1.0, 2.0, 1e-5, 1e-6, 1e-12, 1e-38,
                  1.1920928955078125e-07, 0.125, 10.0):
            self.ccol(v)
        if want("conv"):
            k.scope = 'conv'
            self.ph_conversions()
        if want("input"):
            k.scope = 'input'
            self.ph_input()
            k.fence()
        if want("l0a"):
            k.scope = 'l0a_attn'
            self.ph_attn_block("l0a")
            k.fence()
            k.scope = 'l0a_wo'
            self.ph_wo_residual("l0a")
            k.fence()
            k.scope = 'l0a_epi'
            self.ph_epilogue("l0a")
            k.fence()
        if want("l0f") or want("l0f_attn"):
            k.scope = 'l0f_attn'
            self.ph_attn_block("l0f")
            k.fence()
        if want("l0f") or want("l0f_wo"):
            k.scope = 'l0f_wo'
            self.ph_wo_residual("l0f")
            k.fence()
            k.scope = 'l0f_h2'
            self.ph_ln_aff_A()
            k.fence()
        if want("l0f") or want("l0f_gate"):
            k.scope = 'l0f_gate'
            self.ph_gate_l0f()
            k.fence()
        if want("l0f") or want("l0f_moe"):
            k.scope = 'l0f_moe'
            self.ph_moe("l0f")
            k.fence()
        if want("l0f") or want("l0f_epi"):
            k.scope = 'l0f_epi'
            self.ph_epilogue("l0f")
            k.fence()
        if want("l1a"):
            k.scope = 'l1a_attn'
            self.ph_attn_block("l1a")
            k.fence()
            k.scope = 'l1a_gelu'
            self.ph_gelu_B()
            k.fence()
            k.scope = 'l1a_wo'
            self.ph_wo_residual("l1a")
            k.fence()
            k.scope = 'l1a_epi'
            self.ph_epilogue("l1a")
            k.fence()
        if want("l1f"):
            k.scope = 'l1f_qkv'
            self.ph_l1f_qkv()
            k.fence()
            k.scope = 'l1f_attn'
            self.ph_attn_units_l1f()
            k.fence()
            k.scope = 'l1f_wo'
            self.ph_wo_residual("l1f")
            k.fence()
            k.scope = 'l1f_t2'
            self.ph_t2()
            k.fence()
            k.scope = 'l1f_gate'
            self.ph_gate_l1f()
            k.fence()
            k.scope = 'l1f_moe'
            self.ph_moe("l1f")
            k.fence()
            k.scope = 'l1f_epi'
            self.ph_epilogue("l1f")
            k.fence()
        for dump in getattr(self, "dumps", ()):
            if dump == "A":
                dram_ap = k.dram["dbgA"]
                for dc in range(8):
                    k.op("sync", lambda e, dc=dc, dram_ap=dram_ap: e.dma_start(
                            out=dram_ap[dc], in_=k.A[:, dc, :]),
                         reads=[k.gA[dc]], writes=[k.gOUT[0]], dma=True)
            if dump == "B":
                Bv = self.Bv()
                dram_ap = k.dram["dbgB"]
                for dc in range(8):
                    k.op("sync", lambda e, dc=dc, dram_ap=dram_ap, Bv=Bv: e.dma_start(
                            out=dram_ap[dc], in_=Bv[:, dc, :]),
                         reads=[[self.gBcs(dc, t) for t in range(8)]],
                         writes=[k.gOUT[0]], dma=True)
        k.op("sync", lambda e: e.nop(), reads=[k.gOUT])

    def ph_conversions(self):
        k = self.k
        for src, dst in [("l0f_w1", "w1b_l0"), ("l0f_w2", "w2b_l0"),
                         ("l1f_w1", "w1b_l1"), ("l1f_w3", "w3b_l1"),
                         ("l1f_w2", "w2b_l1")]:
            for i in range(WSHAPES[src][0]):
                k.op("gpsimd",
                     lambda e, s=src, d=dst, i=i: e.dma_start(out=k.dram[d][i], in_=k.dram[s][i]),
                     writes=[k.gWBF[dst][i]], dma=True, hbm_only=True)

    def stats_slice(self, tile_fn, sq_ap, gsq, do_sum=True):
        k = self.k
        ones = k.CON["c_ones"]
        for dc in range(8):
            ap, gr = tile_fn(dc)
            if do_sum:
                k.op("tensor", lambda e, ap=ap, dc=dc: e.matmul(
                        k.PS[0:1, 6, :], ones[:, 0:1], ap,
                        start=(dc == 0), stop=(dc == 7), tile_position=(0, 0)),
                     reads=[gr, k.gCON["c_ones"]],
                     writes=[k.gPS[6]] if dc == 0 else [], rw=[] if dc == 0 else [k.gPS[6]])
            k.op("scalar", lambda e, ap=ap: e.activation(sq_ap, ap, AF.Square),
                 reads=[gr], writes=[gsq])
            k.op("tensor", lambda e, dc=dc: e.matmul(
                    k.PS[0:1, 7, :], ones[:, 0:1], sq_ap,
                    start=(dc == 0), stop=(dc == 7), tile_position=(0, 0)),
                 reads=[gsq, k.gCON["c_ones"]],
                 writes=[k.gPS[7]] if dc == 0 else [], rw=[] if dc == 0 else [k.gPS[7]])

    def ln_rows(self, mode, eps2=None):
        k = self.k
        scl, gs = self.row(0), k.gR[0]
        sft, gf = self.row(1), k.gR[1]
        t0, gt0 = self.row(4), k.gR[4]
        t1, gt1 = self.row(5), k.gR[5]
        c1024 = self.ccol(1.0 / 1024.0)
        gcc = k.gR[7]
        k.op("scalar", lambda e: e.activation(t0, k.PS[0:1, 6, :], AF.Identity, scale=c1024[0:1]),
             reads=[k.gPS[6], gcc], writes=[gt0])
        k.op("scalar", lambda e: e.activation(t1, k.PS[0:1, 7, :], AF.Identity, scale=c1024[0:1]),
             reads=[k.gPS[7], gcc], writes=[gt1])
        k.op("vector", lambda e: e.tensor_mul(sft, t0, t0), reads=[gt0], writes=[gf])
        k.op("vector", lambda e: e.tensor_sub(sft, t1, sft), reads=[gt1], rw=[gf])
        _c = self.ccol(1e-5)
        k.op("scalar", lambda e, _c=_c: e.activation(scl, sft, AF.Ln, bias=_c[0:1]),
             reads=[gf, gcc], writes=[gs])
        _c = self.ccol(-0.5)
        k.op("scalar", lambda e, _c=_c: e.activation(scl, scl, AF.Exp, scale=_c[0:1]),
             reads=[gcc], rw=[gs])
        if mode == "ln2":
            k.op("vector", lambda e: e.tensor_mul(t1, sft, scl), reads=[gf, gs], writes=[gt1])
            k.op("vector", lambda e: e.tensor_mul(t1, t1, scl), reads=[gs], rw=[gt1])
            _c = self.ccol(float(eps2))
            k.op("scalar", lambda e, _c=_c: e.activation(t1, t1, AF.Ln, bias=_c[0:1]),
                 reads=[gcc], rw=[gt1])
            _c = self.ccol(-0.5)
            k.op("scalar", lambda e, _c=_c: e.activation(t1, t1, AF.Exp, scale=_c[0:1]),
                 reads=[gcc], rw=[gt1])
            k.op("vector", lambda e: e.tensor_mul(scl, scl, t1), reads=[gt1], rw=[gs])
        k.op("vector", lambda e: e.tensor_scalar_mul(t0, t0, -1.0), rw=[gt0])
        k.op("vector", lambda e: e.tensor_mul(sft, t0, scl), reads=[gt0, gs], writes=[gf])

    def bcast_f32(self, row_ap, row_g, dst_ap, dst_g, bank=5, n=512):
        k = self.k
        onesf = k.CON["c_onesf"]
        k.op("tensor", lambda e: e.matmul(k.PS[:, bank, 0:n], onesf[0:1, :], row_ap,
                                          start=True, stop=True, tile_position=(0, 0)),
             reads=[row_g, k.gCON["c_onesf"]], writes=[k.gPS[bank]])
        k.op("scalar", lambda e: e.copy(dst_ap, k.PS[:, bank, 0:n]),
             reads=[k.gPS[bank]], writes=[dst_g])

    def ph_input(self):
        k = self.k
        x0, xs, xb = k.dram["x0"], k.dram["xs"], k.dram["xb"]
        xt = [self.Uf(0, 1024), self.Uf(2048, 1024)]
        gxt = [G(), G()]
        stf = self.Uf(4096, 4096).rearrange("p (c t) -> p c t", c=8)
        gstf = G()
        stb = self.Ub(12288, 4096).rearrange("p (c t) -> p c t", c=8)
        gstb = G()
        sqb = self.Ub(16384, 512)
        gsq = G()
        sclb = self.Uf(16896, 512)
        sftb = self.Uf(17920, 512)
        gsclb, gsftb = G(), G()
        ident = k.CON["c_identf"]
        nw, gnw = self.wv("l0a_nw")
        LVL0 = getattr(self, "inlevel", 9)
        for s in range(8):
            for tq in range(4):
                tc = s * 4 + tq
                bi = tc % 2
                k.op("sync", lambda e, tc=tc, bi=bi: e.dma_start(
                        out=xt[bi], in_=x0[tc * 128:(tc + 1) * 128, :]),
                     writes=[gxt[bi]], dma=True)
                if LVL0 <= -3:
                    continue
                for dc in range(8):
                    pb = dc % 2
                    k.op("tensor", lambda e, dc=dc, pb=pb, bi=bi: e.transpose(
                            k.PS[:, pb, 0:128], xt[bi][:, dc * 128:(dc + 1) * 128], ident[:]),
                         reads=[gxt[bi], k.gCON["c_identf"]], writes=[k.gPS[pb]])
                    k.op("scalar", lambda e, dc=dc, pb=pb, tq=tq: e.copy(
                            stf[:, dc, tq * 128:(tq + 1) * 128], k.PS[:, pb, 0:128]),
                         reads=[k.gPS[pb]], writes=[gstf])
                    k.op("vector", lambda e, dc=dc, tq=tq: e.tensor_copy(
                            stb[:, dc, tq * 128:(tq + 1) * 128],
                            stf[:, dc, tq * 128:(tq + 1) * 128]),
                         reads=[gstf], writes=[gstb])
            if LVL0 <= -2:
                continue
            LVL = getattr(self, "inlevel", 9)
            if LVL >= 1:
                self.stats_slice(lambda dc: (stb[:, dc, :], gstb), sqb, gsq)
            if LVL >= 2:
                self.ln_rows("ln2", eps2=1e-6)
            if LVL >= 3:
                self.bcast_f32(self.row(0), k.gR[0], sclb, gsclb)
                self.bcast_f32(self.row(1), k.gR[1], sftb, gsftb)
            for dc in range(8):
                ap = k.A[:, dc, s * 512:(s + 1) * 512]
                if LVL >= 4:
                    k.op("vector", lambda e, ap=ap, dc=dc: e.tensor_mul(ap, stb[:, dc, :], sclb),
                         reads=[gstb, gsclb], writes=[k.gA[dc][s]])
                    k.op("vector", lambda e, ap=ap: e.tensor_add(ap, ap, sftb),
                         reads=[gsftb], rw=[k.gA[dc][s]])
                    k.op("vector", lambda e, ap=ap, dc=dc: e.tensor_scalar(
                            ap, ap, nw[:, dc:dc + 1], None, ALU.mult),
                         reads=[gnw], rw=[k.gA[dc][s]])
                else:
                    k.op("vector", lambda e, ap=ap, dc=dc: e.tensor_copy(ap, stb[:, dc, :]),
                         reads=[gstb], writes=[k.gA[dc][s]])
            k.op("sync", lambda e, s=s: e.dma_start(
                    out=xs.rearrange("c p t -> p c t")[:, :, s * 512:(s + 1) * 512], in_=stf),
                 reads=[gstf], writes=[k.gXS[s]], dma=True)
            k.op("sync", lambda e, s=s: e.dma_start(
                    out=xb.rearrange("c p t -> p c t")[:, :, s * 512:(s + 1) * 512], in_=stb),
                 reads=[gstb], writes=[k.gXB[s]], dma=True)

    def stream_w_piece(self, wname, mc, slot, scale_cols=None, bf_dram=None):
        k = self.k
        view = k.Wt[:, slot * 1024:(slot + 1) * 1024].rearrange("p (c f) -> p c f", c=8)
        gsl = k.gWt[slot]
        src = k.dram[wname][mc]
        k.op("gpsimd", lambda e, src=src, view=view: e.dma_start(
                out=view, in_=src.rearrange("c p f -> p c f")),
             writes=[gsl], dma=True)
        if scale_cols is not None:
            sc, gsc = scale_cols
            for kc in range(8):
                k.op("vector", lambda e, kc=kc, view=view: e.tensor_scalar(
                        view[:, kc, :], view[:, kc, :], sc[:, kc:kc + 1], None, ALU.mult),
                     reads=[gsc], rw=[gsl])
        return view, gsl

    def proj_mc(self, wview, gsl, rhs_fn, evac):
        k = self.k
        for kc in range(8):
            for ts in range(8):
                ap, gr = rhs_fn(kc, ts)
                k.op("tensor", lambda e, ap=ap, kc=kc, ts=ts, wview=wview: e.matmul(
                        k.PS[:, ts, :], wview[:, kc, :], ap,
                        start=(kc == 0), stop=(kc == 7)),
                     reads=[gr, gsl],
                     writes=[k.gPS[ts]] if kc == 0 else [], rw=[] if kc == 0 else [k.gPS[ts]])
        for ts in range(8):
            evac(ts, ts)

    def ph_attn_block(self, blk):
        k = self.k
        p = BLK[blk]
        scale, causal = p["scale"], p["causal"]
        Bv = self.Bv()
        q = self.Ub(0, 4096)
        kk = self.Ub(4096, 4096)
        vfull = self.Ub(8192, 4160)
        v = vfull.rearrange("p (c h f) -> p c h f", c=32, h=2)
        gq, gk, gv = G(), G(), G()
        attnT = self.Ub(12352, 2048).rearrange("p (h j f) -> p h j f", h=2, j=2)
        gaT = garr(2, 2)
        l2b = self.Uf(12352, 512)
        gl2 = [gaT[0][0], gaT[0][1]]
        stage = self.Ub(14400, 4096, parts=64)
        gsq = G()
        sqb = self.Ub(18496, 512)
        gsqb = G()
        vbb = self.Ub(19008, 128)
        gvbb = G()
        rlb = self.Ub(19136, 512, parts=64)
        grlb = G()
        rlbrow = self.Ub(19648, 512, parts=1)
        grlrow = G()
        biasc8 = self.Uf(20160, 8)
        gbias = G()
        qm = self.Uf(20176, 8, parts=2)
        km = self.Uf(20192, 8, parts=2)
        kr = self.Uf(20208, 8, parts=2)
        negB = self.Uf(20224, 8, parts=2)
        negBb = self.Ub(20240, 8, parts=2)
        gqm, gkm, gkr, gnegB, gnegBb = G(), G(), G(), G(), G()
        rlrow = self.row(0)
        grl = k.gR[0]
        negT = self.Uf(20248, 1, parts=8)
        gnegT = G()
        brow = self.Uf(20252, 8, parts=1)
        gbrow = G()
        identf2 = k.CON["c_identf"]
        masks = k.CON["c_masks"]
        ind2 = k.CON["c_ind2"]
        sel2 = k.CON["c_sel2"]
        ones = k.CON["c_ones"]
        csc = self.ccol(1.0)
        cscale = self.ccol(float(scale))
        gcc = k.gR[7]

        if blk == "l0a":
            bqv, gbq = self.wv("l0a_bq")
            bkv, gbk = self.wv("l0a_bk")
            bvv, gbv = self.wv("l0a_bv")
        if blk == "l0f":
            qsv, gqs = self.wv("l0f_qs")
            ksv, gks = self.wv("l0f_ks")

        for pair in range(8):
            for wname, dst, gdst, slot, bias in (
                    (blk + "_wq", q, gq, 0, "l0a_bq" if blk == "l0a" else None),
                    (blk + "_wk", kk, gk, 1, "l0a_bk" if blk == "l0a" else None)):
                wview, gsl = self.stream_w_piece(wname, pair, slot)
                if bias is not None:
                    bt, gbt = self.wv(bias)

                    def evac(ts, b, dst=dst, gdst=gdst, bt=bt, gbt=gbt, pair=pair):
                        k.op("scalar", lambda e, ts=ts, b=b: e.activation(
                                dst[:, ts * 512:(ts + 1) * 512], k.PS[:, b, :], AF.Identity,
                                bias=bt[:, pair:pair + 1]),
                             reads=[k.gPS[b], gbt], writes=[gdst])
                else:
                    def evac(ts, b, dst=dst, gdst=gdst):
                        k.op("scalar", lambda e, ts=ts, b=b: e.copy(
                                dst[:, ts * 512:(ts + 1) * 512], k.PS[:, b, :]),
                             reads=[k.gPS[b]], writes=[gdst])
                self.proj_mc(wview, gsl,
                             lambda kc, ts: (k.A[:, kc, ts * 512:(ts + 1) * 512],
                                             k.gA[kc][ts]), evac)
            wviewv, gslv = self.stream_w_piece(blk + "_wv", pair, 2)
            k.op("vector", lambda e: e.memset(vfull, 1.0), writes=[gv])
            if blk == "l0a":
                bvrow = self.row(4, 128)
                k.op("tensor", lambda e, pair=pair: e.transpose(
                        k.PS[0:1, 4, 0:128], bvv[:, pair:pair + 1],
                        k.CON["c_identf"][:]),
                     reads=[gbv, k.gCON["c_identf"]], writes=[k.gPS[4]])
                k.op("scalar", lambda e, bvrow=bvrow: e.copy(bvrow, k.PS[0:1, 4, 0:128]),
                     reads=[k.gPS[4]], writes=[k.gR[4]])
                k.op("tensor", lambda e, bvrow=bvrow: e.matmul(
                        k.PS[:, 4, 0:128], k.CON["c_onesf"][0:1, :], bvrow,
                        start=True, stop=True, tile_position=(0, 0)),
                     reads=[k.gR[4], k.gCON["c_onesf"]], writes=[k.gPS[4]])
                k.op("scalar", lambda e: e.copy(vbb, k.PS[:, 4, 0:128]),
                     reads=[k.gPS[4]], writes=[gvbb])
            for tc in range(32):
                b = tc % 4
                for kc in range(8):
                    k.op("tensor", lambda e, tc=tc, kc=kc, b=b, wviewv=wviewv: e.matmul(
                            k.PS[:, b, 0:128], k.A[:, kc, tc * 128:(tc + 1) * 128],
                            wviewv[:, kc, :], start=(kc == 0), stop=(kc == 7)),
                         reads=[k.gA[kc][tc // 4], gslv],
                         writes=[k.gPS[b]] if kc == 0 else [], rw=[] if kc == 0 else [k.gPS[b]])
                dstv = v[:, tc, :, 0:64]
                if blk == "l0a":
                    k.op("vector", lambda e, dstv=dstv, b=b: e.tensor_add(
                            dstv, k.PS[:, b, 0:128].rearrange("p (h f) -> p h f", h=2),
                            vbb[:].rearrange("p (h f) -> p h f", h=2)),
                         reads=[k.gPS[b], gvbb], rw=[gv])
                else:
                    k.op("vector", lambda e, dstv=dstv, b=b: e.tensor_copy(
                            dstv, k.PS[:, b, 0:128].rearrange("p (h f) -> p h f", h=2)),
                         reads=[k.gPS[b]], rw=[gv])
            if blk == "l0f":
                for dst, gdst, sclt, gsclt in ((q, gq, qsv, gqs), (kk, gk, ksv, gks)):
                    for ts in range(8):
                        k.op("scalar", lambda e, dst=dst, ts=ts: e.activation(
                                sqb, dst[:, ts * 512:(ts + 1) * 512], AF.Square),
                             reads=[gdst], writes=[gsqb])
                        k.op("tensor", lambda e, ts=ts: e.matmul(
                                k.PS[0:2, 6, :], ind2[:], sqb,
                                start=True, stop=True, tile_position=(0, 0)),
                             reads=[gsqb, k.gCON["c_ind2"]], writes=[k.gPS[6]])
                        r2 = self.row2(4)
                        _c12 = self.ccol(1e-12)
                        _c05n = self.ccol(-0.5)
                        k.op("scalar", lambda e, r2=r2, _c=_c12: e.activation(
                                r2, k.PS[0:2, 6, :], AF.Ln, bias=_c[0:2]),
                             reads=[k.gPS[6], gcc], writes=[k.gR[4]])
                        k.op("scalar", lambda e, r2=r2, _c=_c05n: e.activation(
                                r2, r2, AF.Exp, scale=_c[0:2]),
                             reads=[gcc], rw=[k.gR[4]])
                        k.op("tensor", lambda e, r2=r2: e.matmul(
                                k.PS[:, 5, :], k.CON["c_b2f"][:], r2,
                                start=True, stop=True, tile_position=(0, 0)),
                             reads=[k.gR[4], k.gCON["c_b2f"]], writes=[k.gPS[5]])
                        k.op("scalar", lambda e, sclt=sclt, pair=pair: e.activation(
                                l2b, k.PS[:, 5, :], AF.Identity, scale=sclt[:, pair:pair + 1]),
                             reads=[k.gPS[5], gsclt], writes=[gl2])
                        k.op("vector", lambda e, ts=ts, dst=dst: e.tensor_mul(
                                dst[:, ts * 512:(ts + 1) * 512],
                                dst[:, ts * 512:(ts + 1) * 512], l2b),
                             reads=[gl2], rw=[gdst])
            for src, gsrc, dstm, gdstm in ((q, gq, qm, gqm), (kk, gk, km, gkm)):
                for ts in range(8):
                    k.op("scalar", lambda e, src=src, ts=ts: e.activation(
                            sqb, src[:, ts * 512:(ts + 1) * 512], AF.Square),
                         reads=[gsrc], writes=[gsqb])
                    k.op("tensor", lambda e, ts=ts: e.matmul(
                            k.PS[0:2, 6, :], ind2[:], sqb,
                            start=True, stop=True, tile_position=(0, 0)),
                         reads=[gsqb, k.gCON["c_ind2"]], writes=[k.gPS[6]])
                    k.op("vector", lambda e, ts=ts, dstm=dstm: e.reduce_max(
                            dstm[:, ts:ts + 1], k.PS[0:2, 6, :], axis=AX.X),
                         reads=[k.gPS[6]],
                         writes=[gdstm] if ts == 0 else [], rw=[] if ts == 0 else [gdstm])
            if causal:
                k.op("vector", lambda e: e.tensor_copy(kr[:, 0:1], km[:, 0:1]),
                     reads=[gkm], writes=[gkr])
                for ts in range(1, 8):
                    k.op("vector", lambda e, ts=ts: e.tensor_max(
                            kr[:, ts:ts + 1], kr[:, ts - 1:ts], km[:, ts:ts + 1]),
                         reads=[gkm], rw=[gkr])
            else:
                k.op("vector", lambda e: e.reduce_max(kr[:, 0:1], km[:, 0:8], axis=AX.X),
                     reads=[gkm], writes=[gkr])
                for ts in range(1, 8):
                    k.op("vector", lambda e, ts=ts: e.tensor_copy(kr[:, ts:ts + 1], kr[:, 0:1]),
                         rw=[gkr])
            k.op("vector", lambda e: e.tensor_mul(negB, qm, kr), reads=[gqm, gkr], writes=[gnegB])
            _c = self.ccol(1e-38)
            k.op("scalar", lambda e, _c=_c: e.activation(negB, negB, AF.Ln, bias=_c[0:2]),
                 reads=[gcc], rw=[gnegB])
            _c = self.ccol(0.5)
            k.op("scalar", lambda e, _c=_c: e.activation(negB, negB, AF.Exp, scale=_c[0:2]),
                 reads=[gcc], rw=[gnegB])
            k.op("vector", lambda e: e.tensor_scalar_mul(negB, negB, -float(scale)),
                 rw=[gnegB])
            k.op("tensor", lambda e: e.transpose(k.PS[0:8, 6, 0:2], negB, identf2[0:2, 0:2]),
                 reads=[gnegB, k.gCON["c_identf"]], writes=[k.gPS[6]])
            k.op("vector", lambda e: e.tensor_reduce(negT, k.PS[0:8, 6, 0:2], AX.X, ALU.min),
                 reads=[k.gPS[6]], writes=[gnegT])
            k.op("tensor", lambda e: e.transpose(k.PS[0:1, 6, 0:8], negT, identf2[0:8, 0:8]),
                 reads=[gnegT, k.gCON["c_identf"]], writes=[k.gPS[6]])
            k.op("scalar", lambda e: e.copy(brow, k.PS[0:1, 6, 0:8]),
                 reads=[k.gPS[6]], writes=[gbrow])

            k.op("tensor", lambda e: e.matmul(
                    k.PS[:, 6, 0:8], k.CON["c_onesf"][0:1, :], brow[0:1, 0:8],
                    start=True, stop=True, tile_position=(0, 0)),
                 reads=[gbrow, k.gCON["c_onesf"]], writes=[k.gPS[6]])
            k.op("scalar", lambda e: e.copy(biasc8, k.PS[:, 6, 0:8]),
                 reads=[k.gPS[6]], writes=[gbias])
            for qg in range(8):
                nkt = 4 * qg + 4 if causal else 32
                avb = 4
                def scores_g(g):
                    for h, tp in ((0, (0, 0)), (1, (64, 0))):
                        bnk = 2 * (g % 2) + h
                        k.op("tensor", lambda e, g=g, h=h, tp=tp, bnk=bnk, qg=qg: e.matmul(
                                k.PS[:, bnk, :],
                                kk[h * 64:h * 64 + 64, g * 128:(g + 1) * 128],
                                q[h * 64:h * 64 + 64, qg * 512:(qg + 1) * 512],
                                start=True, stop=True, tile_position=tp),
                             reads=[gk, gq], writes=[k.gPS[bnk]])
                    k.op("scalar", lambda e, g=g, qg=qg: e.activation(
                            attnT[:, :, g % 2, :], k.PS[:, 2 * (g % 2):2 * (g % 2) + 2, :],
                            AF.Exp, bias=biasc8[:, qg:qg + 1], scale=cscale),
                         reads=[k.gPS[2 * (g % 2)], k.gPS[2 * (g % 2) + 1], gbias, gcc],
                         writes=[gaT[0][g % 2], gaT[1][g % 2]])
                    if causal:
                        o = g - 4 * qg
                        if 0 <= o <= 3:
                            for h in (0, 1):
                                k.op("vector", lambda e, g=g, h=h, o=o: e.tensor_mul(
                                        attnT[:, h, g % 2, :], attnT[:, h, g % 2, :],
                                        masks[:, o, :]),
                                     reads=[k.gCON["c_masks"]], rw=[gaT[h][g % 2]])

                def av_g(g, nkt=nkt, avb=avb):
                    for h in (0, 1):
                        k.op("tensor", lambda e, g=g, h=h, nkt=nkt, avb=avb: e.matmul(
                                k.PS[0:65, avb + h, :], v[:, g, h, :], attnT[:, h, g % 2, :],
                                start=(g == 0), stop=(g == nkt - 1)),
                             reads=[gv, gaT[h][g % 2]],
                             writes=[k.gPS[avb + h]] if g == 0 else [],
                             rw=[] if g == 0 else [k.gPS[avb + h]])

                scores_g(0)
                for g in range(1, nkt):
                    scores_g(g)
                    av_g(g - 1)
                av_g(nkt - 1)
                for h in (0, 1):
                    k.op("scalar", lambda e, h=h, avb=avb: e.activation(
                            rlrow, k.PS[64:65, avb + h, :], AF.Ln),
                         reads=[k.gPS[avb + h]], writes=[grl])
                    k.op("scalar", lambda e: e.activation(rlbrow, rlrow, AF.Exp,
                                                          scale=self.ccol(-1.0)[0:1]),
                         reads=[grl, gcc], writes=[grlrow])
                    k.op("tensor", lambda e: e.matmul(k.PS[0:64, 6, :], ones[0:1, 0:64], rlbrow,
                                                      start=True, stop=True, tile_position=(0, 0)),
                         reads=[grlrow, k.gCON["c_ones"]], writes=[k.gPS[6]])
                    k.op("scalar", lambda e: e.copy(rlb, k.PS[0:64, 6, :]),
                         reads=[k.gPS[6]], writes=[grlb])
                    if h == 0:
                        k.op("vector", lambda e, pair=pair, qg=qg, avb=avb: e.tensor_mul(
                                Bv[0:64, pair, qg * 512:(qg + 1) * 512], k.PS[0:64, avb, :], rlb),
                             reads=[k.gPS[avb], grlb], writes=[self.gBcs(pair, qg)])
                    else:
                        k.op("vector", lambda e, qg=qg, avb=avb: e.tensor_mul(
                                stage[:, qg * 512:(qg + 1) * 512], k.PS[0:64, avb + 1, :], rlb),
                             reads=[k.gPS[avb + 1], grlb], rw=[gsq])
            k.op("gpsimd", lambda e, pair=pair: e.dma_start(
                    out=Bv[64:128, pair, :], in_=stage),
                 reads=[gsq], writes=[[self.gBcs(pair, t) for t in range(8)]], dma=True)

    def ph_gelu_B(self):
        k = self.k
        Bv = self.Bv()
        for dc in range(8):
            for ts in range(8):
                k.op("scalar", lambda e, dc=dc, ts=ts: e.activation(
                        Bv[:, dc, ts * 512:(ts + 1) * 512],
                        Bv[:, dc, ts * 512:(ts + 1) * 512], AF.Gelu),
                     rw=[self.gBcs(dc, ts)])

    def ph_wo_residual(self, blk):
        k = self.k
        Bv = self.Bv()
        self.gwo_tmp = G()
        if blk == "l0a":
            bov, gbo = self.wv("l0a_bo")
        for mc in range(8):
            wview, gsl = self.stream_w_piece(blk + "_wo", mc, 3 + (mc % 2))

            if blk == "l0a":
                def evac(ts, b, mc=mc):
                    ap = k.A[:, mc, ts * 512:(ts + 1) * 512]
                    wt = self.Ub(0, 512)
                    k.op("scalar", lambda e, b=b, mc=mc, wt=wt: e.activation(
                            wt, k.PS[:, b, :], AF.Identity, bias=bov[:, mc:mc + 1]),
                         reads=[k.gPS[b], gbo], writes=[self.gwo_tmp])
                    k.op("vector", lambda e, ap=ap, wt=wt: e.tensor_add(ap, ap, wt),
                         reads=[self.gwo_tmp], rw=[k.gA[mc][ts]])
            elif blk == "l0f":
                def evac(ts, b, mc=mc):
                    ap = k.A[:, mc, ts * 512:(ts + 1) * 512]
                    k.op("scalar", lambda e, ap=ap, b=b: e.copy(ap, k.PS[:, b, :]),
                         reads=[k.gPS[b]], writes=[k.gA[mc][ts]])
            else:
                def evac(ts, b, mc=mc):
                    ap = k.A[:, mc, ts * 512:(ts + 1) * 512]
                    k.op("vector", lambda e, ap=ap, b=b: e.tensor_add(ap, k.PS[:, b, :], ap),
                         reads=[k.gPS[b]], rw=[k.gA[mc][ts]])
            self.proj_mc(wview, gsl,
                         lambda kc, ts: (Bv[:, kc, ts * 512:(ts + 1) * 512],
                                         self.gBcs(kc, ts)), evac)
        if blk == "l1f":
            for dc in range(8):
                k.op("sync", lambda e, dc=dc: e.dma_start(
                        out=k.dram["ub"][dc], in_=k.A[:, dc, :]),
                     reads=[k.gA[dc]], writes=[k.gUB[dc]], dma=True)

    def ph_ln_aff_A(self):
        k = self.k
        gv_, gg = self.wv("l0f_g")
        bv_, gb = self.wv("l0f_b")
        sqb = self.Ub(0, 512)
        gsq = G()
        sclb = self.Uf(512, 512)
        sftb = self.Uf(1536, 512)
        gsclb, gsftb = G(), G()
        for s in range(8):
            self.stats_slice(lambda dc, s=s: (k.A[:, dc, s * 512:(s + 1) * 512],
                                              k.gA[dc][s]), sqb, gsq)
            self.ln_rows("ln")
            self.bcast_f32(self.row(0), k.gR[0], sclb, gsclb)
            self.bcast_f32(self.row(1), k.gR[1], sftb, gsftb)
            for dc in range(8):
                ap = k.A[:, dc, s * 512:(s + 1) * 512]
                k.op("vector", lambda e, ap=ap: e.tensor_mul(ap, ap, sclb),
                     reads=[gsclb], rw=[k.gA[dc][s]])
                k.op("vector", lambda e, ap=ap: e.tensor_add(ap, ap, sftb),
                     reads=[gsftb], rw=[k.gA[dc][s]])
                k.op("vector", lambda e, ap=ap, dc=dc: e.tensor_scalar(
                        ap, ap, gv_[:, dc:dc + 1], bv_[:, dc:dc + 1], ALU.mult, ALU.add),
                     reads=[gg, gb], rw=[k.gA[dc][s]])

    def ph_gate_l0f(self):
        k = self.k
        gbv, ggb = self.wv("l0f_gateb")
        wv_b = k.Wt[:, 0:64].rearrange("p (c f) -> p c f", c=8)
        gslb = k.gWt[0]
        k.op("gpsimd", lambda e: e.dma_start(
                out=wv_b, in_=k.dram["l0f_gatew"].rearrange("c p f -> p c f")),
             writes=[gslb], dma=True)
        glog = self.Uf(0, 512, parts=8)
        gglog = G()
        et = self.Uf(1024, 8)
        get = G()
        sm = self.Uf(1040, 1)
        gsm = G()
        grows = k.R[0:8, 0:2048].bitcast(BF16)
        ggrows = G()
        identf = k.CON["c_identf"]
        for s in range(8):
            for kc in range(8):
                k.op("tensor", lambda e, kc=kc, s=s: e.matmul(
                        k.PS[0:8, 6, :], wv_b[:, kc, :], k.A[:, kc, s * 512:(s + 1) * 512],
                        start=(kc == 0), stop=(kc == 7), tile_position=(0, 0)),
                     reads=[k.gA[kc][s], gslb],
                     writes=[k.gPS[6]] if kc == 0 else [], rw=[] if kc == 0 else [k.gPS[6]])
            k.op("scalar", lambda e: e.activation(glog, k.PS[0:8, 6, :], AF.Identity,
                                                  bias=gbv[0:8, 0:1]),
                 reads=[k.gPS[6], ggb], writes=[gglog])
            for i in range(4):
                tt = s * 4 + i
                k.op("tensor", lambda e, i=i: e.transpose(
                        k.PS[:, 5, 0:8], glog[:, i * 128:(i + 1) * 128], identf[0:8, 0:8]),
                     reads=[gglog, k.gCON["c_identf"]], writes=[k.gPS[5]])
                k.op("vector", lambda e: e.reduce_max(sm, k.PS[:, 5, 0:8], axis=AX.X,
                                                      negate=True),
                     reads=[k.gPS[5]], writes=[gsm])
                k.op("scalar", lambda e: e.activation(et, k.PS[:, 5, 0:8], AF.Exp, bias=sm),
                     reads=[k.gPS[5], gsm], writes=[get])
                k.op("vector", lambda e: e.reduce_sum(sm, et, axis=AX.X),
                     reads=[get], writes=[gsm])
                k.op("vector", lambda e: e.reciprocal(sm, sm), rw=[gsm])
                k.op("vector", lambda e: e.tensor_scalar(et, et, sm, None, ALU.mult),
                     reads=[gsm], rw=[get])
                k.op("tensor", lambda e: e.transpose(k.PS[0:8, 5, 0:128], et, identf[:]),
                     reads=[get, k.gCON["c_identf"]], writes=[k.gPS[5]])
                k.op("scalar", lambda e, tt=tt: e.copy(
                        grows[:, tt * 128:(tt + 1) * 128], k.PS[0:8, 5, 0:128]),
                     reads=[k.gPS[5]],
                     writes=[ggrows] if tt == 0 else [], rw=[] if tt == 0 else [ggrows])
        self.grows, self.ggrows = grows, ggrows

    def ph_gate_l1f(self):
        k = self.k
        wv_b = k.Wt[:, 0:32].rearrange("p (c f) -> p c f", c=8)
        gslb = k.gWt[0]
        k.op("gpsimd", lambda e: e.dma_start(
                out=wv_b, in_=k.dram["l1f_gatew"].rearrange("c p f -> p c f")),
             writes=[gslb], dma=True)
        glog = self.Uf(0, 512, parts=4)
        gglog = G()
        et = self.Uf(1024, 4)
        get = G()
        sm = self.Uf(1032, 1)
        s2c = self.Uf(1034, 1)
        tca = self.Uf(1036, 1)
        tcb = self.Uf(1038, 1)
        msk = self.Uf(1044, 4)
        gsm, gs2, gta, gtb, gmsk = G(), G(), G(), G(), G()
        grows = k.R[0:4, 0:2048].bitcast(BF16)
        ggrows = G()
        identf = k.CON["c_identf"]
        for s in range(8):
            for kc in range(8):
                k.op("tensor", lambda e, kc=kc, s=s: e.matmul(
                        k.PS[0:4, 6, :], wv_b[:, kc, :], k.A[:, kc, s * 512:(s + 1) * 512],
                        start=(kc == 0), stop=(kc == 7), tile_position=(0, 0)),
                     reads=[k.gA[kc][s], gslb],
                     writes=[k.gPS[6]] if kc == 0 else [], rw=[] if kc == 0 else [k.gPS[6]])
            k.op("scalar", lambda e: e.copy(glog, k.PS[0:4, 6, :]),
                 reads=[k.gPS[6]], writes=[gglog])
            for i in range(4):
                tt = s * 4 + i
                k.op("tensor", lambda e, i=i: e.transpose(
                        k.PS[:, 5, 0:4], glog[:, i * 128:(i + 1) * 128], identf[0:4, 0:4]),
                     reads=[gglog, k.gCON["c_identf"]], writes=[k.gPS[5]])
                k.op("vector", lambda e: e.reduce_max(sm, k.PS[:, 5, 0:4], axis=AX.X,
                                                      negate=True),
                     reads=[k.gPS[5]], writes=[gsm])
                k.op("scalar", lambda e: e.activation(et, k.PS[:, 5, 0:4], AF.Exp, bias=sm),
                     reads=[k.gPS[5], gsm], writes=[get])
                k.op("vector", lambda e: e.reduce_sum(sm, et, axis=AX.X),
                     reads=[get], writes=[gsm])
                k.op("vector", lambda e: e.reciprocal(sm, sm), rw=[gsm])
                k.op("vector", lambda e: e.tensor_scalar(et, et, sm, None, ALU.mult),
                     reads=[gsm], rw=[get])
                k.op("vector", lambda e: e.reduce_max(sm, et, axis=AX.X),
                     reads=[get], writes=[gsm])
                k.op("vector", lambda e: e.tensor_max(tca, et[:, 0:1], et[:, 1:2]),
                     reads=[get], writes=[gta])
                k.op("vector", lambda e: e.tensor_tensor(tcb, et[:, 0:1], et[:, 1:2], ALU.min),
                     reads=[get], writes=[gtb])
                k.op("vector", lambda e: e.tensor_tensor(s2c, et[:, 2:3], et[:, 3:4], ALU.min),
                     reads=[get], writes=[gs2])
                k.op("vector", lambda e: e.tensor_max(tcb, tcb, s2c), reads=[gs2], rw=[gtb])
                k.op("vector", lambda e: e.tensor_max(s2c, et[:, 2:3], et[:, 3:4]),
                     reads=[get], rw=[gs2])
                k.op("vector", lambda e: e.tensor_tensor(tca, tca, s2c, ALU.min),
                     reads=[gs2], rw=[gta])
                k.op("vector", lambda e: e.tensor_max(s2c, tca, tcb), reads=[gta, gtb], rw=[gs2])
                k.op("vector", lambda e: e.tensor_scalar(msk, et, s2c, None, ALU.is_ge),
                     reads=[get, gs2], writes=[gmsk])
                k.op("vector", lambda e: e.tensor_mul(msk, msk, et), reads=[get], rw=[gmsk])
                k.op("vector", lambda e: e.tensor_add(sm, sm, s2c), reads=[gs2], rw=[gsm])
                k.op("vector", lambda e: e.reciprocal(sm, sm), rw=[gsm])
                k.op("vector", lambda e: e.tensor_scalar(msk, msk, sm, None, ALU.mult),
                     reads=[gsm], rw=[gmsk])
                k.op("tensor", lambda e: e.transpose(k.PS[0:4, 5, 0:128], msk, identf[:]),
                     reads=[gmsk, k.gCON["c_identf"]], writes=[k.gPS[5]])
                k.op("scalar", lambda e, tt=tt: e.copy(
                        grows[:, tt * 128:(tt + 1) * 128], k.PS[0:4, 5, 0:128]),
                     reads=[k.gPS[5]],
                     writes=[ggrows] if tt == 0 else [], rw=[] if tt == 0 else [ggrows])
        self.comb_rows, self.gcomb = grows, ggrows

    def ph_moe(self, blk):
        k = self.k
        l0 = blk == "l0f"
        E, FC = (8, 32) if l0 else (4, 24)
        w1b, w2b = ("w1b_l0", "w2b_l0") if l0 else ("w1b_l1", "w2b_l1")
        sel8 = k.CON["c_sel8"]
        acc = self.Uf(0, 8192).rearrange("p (c t) -> p c t", c=8)
        gacc = garr(8)
        gbt = [self.Ub(16384, 512), self.Ub(16896, 512)]
        ggbt = [G(), G()]
        tmp = self.Uf(17408, 512)
        gtmp = G()
        s1t = self.Ub(18432, 1024).rearrange("p (j f) -> p j f", j=2)
        gs1 = G()
        b2bf = self.Ub(19456, 1024, parts=8).rearrange("p (c f) -> p c f", c=8)
        gb2bf = G()
        hid = self.Bh()
        moeb = k.dram["moeb"]
        w1slot = [k.Wt[:, 0:1024].rearrange("p (c f) -> p c f", c=8),
                  k.Wt[:, 1024:2048].rearrange("p (c f) -> p c f", c=8)]
        w1big = [k.Wt[:, 0:2048].rearrange("p (g c f) -> p g c f", g=2, c=8),
                 k.Wt[:, 2048:4096].rearrange("p (g c f) -> p g c f", g=2, c=8)]
        gw1big = [k.gWt[0], k.gWt[2]]
        w3slot = [k.Wt[:, 2048:3072].rearrange("p (c f) -> p c f", c=8),
                  k.Wt[:, 3072:4096].rearrange("p (c f) -> p c f", c=8)]
        w2slot = [k.Wt[:, 4096:5120].rearrange("p (c f) -> p c f", c=8),
                  k.Wt[:, 5120:6144].rearrange("p (c f) -> p c f", c=8)]
        gw1 = [k.gWt[0], k.gWt[1]]
        gw3 = [k.gWt[2], k.gWt[3]]
        gw2 = [k.gWt[4], k.gWt[5]]
        grows, ggrows = (self.grows, self.ggrows) if l0 else (self.comb_rows, self.gcomb)
        nrow = 8 if l0 else 4
        if l0:
            b1v, gb1 = self.wv("l0f_b1")
            k.op("gpsimd", lambda e: e.dma_start(out=b2bf[:], in_=k.dram["l0f_b2"]),
                 writes=[gb2bf], dma=True)

        for qq in range(4):
            tb = qq * 1024
            for ei in range(E):
                for fc in range(FC):
                    sl = fc % 2
                    if l0:
                        if fc % 2 == 0:
                            sl2 = (fc // 2) % 2
                            k.op("gpsimd", lambda e, ei=ei, fc=fc, sl2=sl2: e.dma_start(
                                    out=w1big[sl2],
                                    in_=k.dram[w1b][ei, fc:fc + 2].rearrange("g c p f -> p g c f")),
                                 reads=[k.gWBF[w1b][ei]], writes=[gw1big[sl2]], dma=True)
                    else:
                        k.op("gpsimd", lambda e, ei=ei, fc=fc, sl=sl: e.dma_start(
                                out=w1slot[sl], in_=k.dram[w1b][ei, fc].rearrange("c p f -> p c f")),
                             reads=[k.gWBF[w1b][ei]], writes=[gw1[sl]], dma=True)
                    if not l0:
                        k.op("gpsimd", lambda e, ei=ei, fc=fc, sl=sl: e.dma_start(
                                out=w3slot[sl],
                                in_=k.dram["w3b_l1"][ei, fc].rearrange("c p f -> p c f")),
                             reads=[k.gWBF["w3b_l1"][ei]], writes=[gw3[sl]], dma=True)
                    hb = 2 * (fc % 2)
                    wb3 = 4 + 2 * (fc % 2)
                    for half in (0, 1):
                        for kc in range(8):
                            if l0:
                                sl2 = (fc // 2) % 2
                                wap = w1big[sl2][:, fc % 2, kc, :]
                                gwap = gw1big[sl2]
                            else:
                                wap = w1slot[sl][:, kc, :]
                                gwap = gw1[sl]
                            k.op("tensor", lambda e, kc=kc, half=half, tb=tb, hb=hb, wap=wap: e.matmul(
                                    k.PS[:, hb + half, :], wap,
                                    k.A[:, kc, tb + half * 512:tb + half * 512 + 512],
                                    start=(kc == 0), stop=(kc == 7)),
                                 reads=[k.gA[kc][qq * 2 + half], gwap],
                                 writes=[k.gPS[hb + half]] if kc == 0 else [],
                                 rw=[] if kc == 0 else [k.gPS[hb + half]])
                            if not l0:
                                k.op("tensor", lambda e, kc=kc, half=half, sl=sl, tb=tb, wb3=wb3: e.matmul(
                                        k.PS[:, wb3 + half, :], w3slot[sl][:, kc, :],
                                        k.A[:, kc, tb + half * 512:tb + half * 512 + 512],
                                        start=(kc == 0), stop=(kc == 7)),
                                     reads=[k.gA[kc][qq * 2 + half], gw3[sl]],
                                     writes=[k.gPS[wb3 + half]] if kc == 0 else [],
                                     rw=[] if kc == 0 else [k.gPS[wb3 + half]])
                    if l0:
                        k.op("scalar", lambda e, fc=fc, ei=ei, hb=hb: e.activation(
                                hid[:, fc, :], k.PS[:, hb:hb + 2, :], AF.Gelu_apprx_tanh,
                                bias=b1v[:, ei, fc:fc + 1]),
                             reads=[k.gPS[hb], k.gPS[hb + 1], gb1],
                             writes=[self.gBr(fc * 1024, 1024)])
                    else:
                        k.op("scalar", lambda e, fc=fc, hb=hb: e.activation(
                                s1t[:], k.PS[:, hb:hb + 2, :], AF.Silu),
                             reads=[k.gPS[hb], k.gPS[hb + 1]], writes=[gs1])
                        k.op("vector", lambda e, fc=fc, wb3=wb3: e.tensor_mul(
                                hid[:, fc, :].rearrange("p (j f) -> p j f", j=2),
                                k.PS[:, wb3:wb3 + 2, :], s1t[:]),
                             reads=[k.gPS[wb3], k.gPS[wb3 + 1], gs1],
                             writes=[self.gBr(fc * 1024, 1024)])
                for half in (0, 1):
                    k.op("tensor", lambda e, ei=ei, half=half, tb=tb: e.matmul(
                            k.PS[:, 6, :], sel8[0:nrow, ei, :],
                            grows[0:nrow, tb + half * 512:tb + half * 512 + 512],
                            start=True, stop=True, tile_position=(0, 0)),
                         reads=[ggrows, k.gCON["c_sel8"]], writes=[k.gPS[6]])
                    k.op("scalar", lambda e, half=half: e.copy(gbt[half], k.PS[:, 6, :]),
                         reads=[k.gPS[6]], writes=[ggbt[half]])
                for mc in range(8):
                    nsub = FC // 8
                    ob = 4 + 2 * (mc % 2)
                    for sub in range(nsub):
                        k.op("gpsimd", lambda e, ei=ei, mc=mc, sub=sub: e.dma_start(
                                out=w2slot[sub % 2],
                                in_=k.dram[w2b][ei, mc, sub * 8:(sub + 1) * 8].rearrange(
                                    "c p f -> p c f")),
                             reads=[k.gWBF[w2b][ei]], writes=[gw2[sub % 2]], dma=True)
                        for half in (0, 1):
                            for fi in range(8):
                                fc = sub * 8 + fi
                                k.op("tensor", lambda e, fc=fc, half=half, sub=sub, tb=tb, ob=ob: e.matmul(
                                        k.PS[:, ob + half, :], w2slot[sub % 2][:, fc % 8, :],
                                        hid[:, fc, half * 512:(half + 1) * 512],
                                        start=(fc == 0), stop=(fc == FC - 1)),
                                     reads=[self.gBr(fc * 1024 + half * 512, 512), gw2[sub % 2]],
                                     writes=[k.gPS[ob + half]] if fc == 0 else [],
                                     rw=[] if fc == 0 else [k.gPS[ob + half]])
                    for half in (0, 1):
                        accv = acc[:, mc, half * 512:(half + 1) * 512]
                        if ei == 0:
                            k.op("vector", lambda e, accv=accv, half=half, ob=ob: e.tensor_mul(
                                    accv, k.PS[:, ob + half, :], gbt[half]),
                                 reads=[k.gPS[ob + half], ggbt[half]],
                                 writes=[gacc[mc]] if half == 0 else [],
                                 rw=[] if half == 0 else [gacc[mc]])
                        else:
                            k.op("vector", lambda e, half=half, ob=ob: e.tensor_mul(
                                    tmp, k.PS[:, ob + half, :], gbt[half]),
                                 reads=[k.gPS[ob + half], ggbt[half]], writes=[gtmp])
                            k.op("vector", lambda e, accv=accv: e.tensor_add(accv, accv, tmp),
                                 reads=[gtmp], rw=[gacc[mc]])
            if l0:
                for mc in range(8):
                    for half in (0, 1):
                        k.op("tensor", lambda e, mc=mc, half=half, tb=tb: e.matmul(
                                k.PS[:, 6, :], b2bf[:, mc, :],
                                grows[0:8, tb + half * 512:tb + half * 512 + 512],
                                start=True, stop=True, tile_position=(0, 0)),
                             reads=[gb2bf, ggrows], writes=[k.gPS[6]])
                        k.op("vector", lambda e, mc=mc, half=half: e.tensor_add(
                                acc[:, mc, half * 512:(half + 1) * 512],
                                acc[:, mc, half * 512:(half + 1) * 512], k.PS[:, 6, :]),
                             reads=[k.gPS[6]], rw=[gacc[mc]])
            for mc in range(8):
                k.op("vector", lambda e, mc=mc: e.tensor_copy(hid[:, mc, :], acc[:, mc, :]),
                     reads=[gacc[mc]], writes=[self.gBr(mc * 1024, 1024)])
                k.op("sync", lambda e, mc=mc, tb=tb: e.dma_start(
                        out=moeb[mc][:, tb:tb + 1024], in_=hid[:, mc, :]),
                     reads=[self.gBr(mc * 1024, 1024)], writes=[k.gMOEB[mc]], dma=True)

    def ph_l1f_qkv(self):
        k = self.k
        Bv = self.Bv()
        g1v, gg1 = self.wv("l1f_g1")
        sqb = self.Ub(0, 512)
        gsq = G()
        rstdb = self.Uf(512, 512)
        grstdb = G()
        rcol = self.Uf(1552, 4)
        grcol = G()
        ptmp = self.Ub(1568, 512)
        gptmp = G()
        ratile = self.Ub(2080, 512)
        gra = G()
        vts = self.Ub(2576, 8192).rearrange("p (c m f) -> p c m f", c=32, m=2)
        gvts = [G(), G()]
        cosb = self.Ub(10768, 4096)
        sinb = self.Ub(14864, 4096)
        gcos, gsin = G(), G()
        identf = k.CON["c_identf"]
        ropeP = k.CON["c_ropeP"]
        k.op("sync", lambda e: e.dma_start(out=cosb, in_=k.dram["c_cos"]),
             writes=[gcos], dma=True)
        k.op("sync", lambda e: e.dma_start(out=sinb, in_=k.dram["c_sin"]),
             writes=[gsin], dma=True)
        for ts in range(8):
            self.stats_slice(lambda dc, ts=ts: (k.A[:, dc, ts * 512:(ts + 1) * 512],
                                                k.gA[dc][ts]), sqb, gsq, do_sum=False)
            rr, grr = self.row(0), k.gR[0]
            k.op("scalar", lambda e, rr=rr: e.activation(
                    rr, k.PS[0:1, 7, :], AF.Identity, scale=self.ccol(1.0 / 1024.0)[0:1]),
                 reads=[k.gPS[7], k.gR[7]], writes=[grr])
            k.op("scalar", lambda e, rr=rr: e.activation(
                    rr, rr, AF.Ln, bias=self.ccol(1.1920928955078125e-07)[0:1]),
                 reads=[k.gR[7]], rw=[grr])
            k.op("scalar", lambda e, rr=rr: e.activation(
                    rr, rr, AF.Exp, scale=self.ccol(-0.5)[0:1]),
                 reads=[k.gR[7]], rw=[grr])
            self.bcast_f32(rr, grr, rstdb, grstdb)
            for i in range(4):
                k.op("tensor", lambda e, i=i, rr=rr: e.transpose(
                        k.PS[0:128, 5, 0:1], rr[0:1, i * 128:(i + 1) * 128], identf[0:1, 0:1]),
                     reads=[grr, k.gCON["c_identf"]], writes=[k.gPS[5]])
                k.op("scalar", lambda e, i=i: e.copy(rcol[:, i:i + 1], k.PS[0:128, 5, 0:1]),
                     reads=[k.gPS[5]],
                     writes=[grcol] if i == 0 else [], rw=[] if i == 0 else [grcol])
            for wname, nmc, is_q in (("l1f_wq", 8, True), ("l1f_wk", 2, False)):
                for mc in range(nmc):
                    slot = mc % 2
                    wview, gsl = self.stream_w_piece(wname, mc, slot,
                                                     scale_cols=(g1v, gg1))
                    for kc in range(8):
                        k.op("tensor", lambda e, kc=kc, ts=ts, wview=wview: e.matmul(
                                k.PS[:, 6, :], wview[:, kc, :],
                                k.A[:, kc, ts * 512:(ts + 1) * 512],
                                start=(kc == 0), stop=(kc == 7)),
                             reads=[k.gA[kc][ts], gsl],
                             writes=[k.gPS[6]] if kc == 0 else [],
                             rw=[] if kc == 0 else [k.gPS[6]])
                    k.op("vector", lambda e: e.tensor_mul(ptmp, k.PS[:, 6, :], rstdb),
                         reads=[k.gPS[6], grstdb], writes=[gptmp])
                    k.op("tensor", lambda e: e.matmul(k.PS[:, 4, :], ropeP[:], ptmp,
                                                      start=True, stop=True),
                         reads=[gptmp, k.gCON["c_ropeP"]], writes=[k.gPS[4]])
                    cs = cosb[:, ts * 512:(ts + 1) * 512]
                    sn = sinb[:, ts * 512:(ts + 1) * 512]
                    k.op("vector", lambda e, cs=cs: e.tensor_mul(ratile, ptmp, cs),
                         reads=[gptmp, gcos], writes=[gra])
                    if is_q:
                        dst = Bv[:, mc, ts * 512:(ts + 1) * 512]
                        gdst = self.gBcs(mc, ts)
                    else:
                        dst = ptmp
                        gdst = gptmp
                    k.op("vector", lambda e, sn=sn, dst=dst: e.tensor_mul(
                            dst, k.PS[:, 4, :], sn),
                         reads=[k.gPS[4], gsin],
                         writes=[gdst] if is_q else [], rw=[] if is_q else [gdst])
                    k.op("vector", lambda e, dst=dst: e.tensor_add(dst, dst, ratile),
                         reads=[gra], rw=[gdst])
                    if not is_q:
                        k.op("sync", lambda e, mc=mc, ts=ts: e.dma_start(
                                out=k.dram["kvb"][mc][:, ts * 512:(ts + 1) * 512], in_=ptmp),
                             reads=[gptmp], writes=[k.gKVB[mc]], dma=True)
            for mc in range(2):
                wview, gsl = self.stream_w_piece("l1f_wv", mc, 2 + mc,
                                                 scale_cols=(g1v, gg1))
                for tq in range(4):
                    tc = ts * 4 + tq
                    b = tq % 2
                    for kc in range(8):
                        k.op("tensor", lambda e, tc=tc, kc=kc, b=b, wview=wview: e.matmul(
                                k.PS[:, b, 0:128], k.A[:, kc, tc * 128:(tc + 1) * 128],
                                wview[:, kc, :], start=(kc == 0), stop=(kc == 7)),
                             reads=[k.gA[kc][ts], gsl],
                             writes=[k.gPS[b]] if kc == 0 else [],
                             rw=[] if kc == 0 else [k.gPS[b]])
                    k.op("scalar", lambda e, tc=tc, b=b, tq=tq, mc=mc: e.activation(
                            vts[:, tc, mc, :], k.PS[:, b, 0:128], AF.Identity,
                            scale=rcol[:, tq:tq + 1]),
                         reads=[k.gPS[b], grcol], rw=[gvts[mc]])
        for mc in range(2):
            for h in (0, 1):
                k.op("sync", lambda e, mc=mc, h=h: e.dma_start(
                        out=k.dram["vtb"][2 * mc + h],
                        in_=vts[:, :, mc, h * 64:(h + 1) * 64]),
                     reads=[gvts[mc]], writes=[k.gVTB[2 * mc + h]], dma=True)

    def ph_attn_units_l1f(self):
        k = self.k
        scale = 0.125
        Bv = self.Bv()
        kdup = self.Ub(0, 4096)
        gkd = G()
        vfull = self.Ub(4096, 2080)
        v = vfull.rearrange("p (c f) -> p c f", c=32)
        gv = G()
        attnT = self.Ub(6176, 2048).rearrange("p (h j f) -> p h j f", h=2, j=2)
        gaT = garr(2, 2)
        sqb = self.Ub(8224, 512)
        gsqb = G()
        stage = self.Ub(8736, 4096, parts=64)
        gsq = G()
        rlb = self.Ub(12832, 512, parts=64)
        grlb = G()
        rlbrow = self.Ub(13344, 512, parts=1)
        grlrow = G()
        biasc8 = self.Uf(13856, 8)
        gbias = G()
        qm = self.Uf(13872, 8, parts=2)
        km = self.Uf(13888, 8, parts=2)
        kr = self.Uf(13904, 8, parts=2)
        negB = self.Uf(13920, 8, parts=2)
        negBb = self.Ub(13936, 8, parts=2)
        gqm, gkm, gkr, gnegB, gnegBb = G(), G(), G(), G(), G()
        rlrow = self.row(0)
        grl = k.gR[0]
        negT = self.Uf(13944, 1, parts=8)
        gnegT = G()
        brow = self.Uf(13948, 8, parts=1)
        gbrow = G()
        identf = k.CON["c_identf"]
        masks = k.CON["c_masks"]
        ind2 = k.CON["c_ind2"]
        sel2 = k.CON["c_sel2"]
        ones = k.CON["c_ones"]
        cscale = self.ccol(float(scale))
        gcc = k.gR[7]
        for gidx in range(4):
            src = k.dram["kvb"][gidx // 2]
            ro = (gidx % 2) * 64
            k.op("sync", lambda e, src=src, ro=ro: e.dma_start(
                    out=kdup[0:64, :], in_=src[ro:ro + 64, :]),
                 reads=[k.gKVB[gidx // 2]], writes=[gkd], dma=True)
            k.op("sync", lambda e, src=src, ro=ro: e.dma_start(
                    out=self.Ub(0, 4096)[64:128, :], in_=src[ro:ro + 64, :]),
                 reads=[k.gKVB[gidx // 2]], rw=[gkd], dma=True)
            k.op("vector", lambda e: e.memset(vfull, 1.0), writes=[gv])
            k.op("sync", lambda e, gidx=gidx: e.dma_start(
                    out=v[:, :, 0:64], in_=k.dram["vtb"][gidx]),
                 reads=[k.gVTB[gidx]], rw=[gv], dma=True)
            for ts in range(8):
                k.op("scalar", lambda e, ts=ts: e.activation(
                        sqb, kdup[:, ts * 512:(ts + 1) * 512], AF.Square),
                     reads=[gkd], writes=[gsqb])
                k.op("tensor", lambda e, ts=ts: e.matmul(
                        k.PS[0:2, 6, :], ind2[:], sqb,
                        start=True, stop=True, tile_position=(0, 0)),
                     reads=[gsqb, k.gCON["c_ind2"]], writes=[k.gPS[6]])
                k.op("vector", lambda e, ts=ts: e.reduce_max(
                        km[:, ts:ts + 1], k.PS[0:2, 6, :], axis=AX.X),
                     reads=[k.gPS[6]],
                     writes=[gkm] if ts == 0 else [], rw=[] if ts == 0 else [gkm])
            k.op("vector", lambda e: e.tensor_copy(kr[:, 0:1], km[:, 0:1]),
                 reads=[gkm], writes=[gkr])
            for ts in range(1, 8):
                k.op("vector", lambda e, ts=ts: e.tensor_max(
                        kr[:, ts:ts + 1], kr[:, ts - 1:ts], km[:, ts:ts + 1]),
                     reads=[gkm], rw=[gkr])
            for pair in (2 * gidx, 2 * gidx + 1):
                for ts in range(8):
                    k.op("scalar", lambda e, pair=pair, ts=ts: e.activation(
                            sqb, Bv[:, pair, ts * 512:(ts + 1) * 512], AF.Square),
                         reads=[self.gBcs(pair, ts)], writes=[gsqb])
                    k.op("tensor", lambda e, ts=ts: e.matmul(
                            k.PS[0:2, 6, :], ind2[:], sqb,
                            start=True, stop=True, tile_position=(0, 0)),
                         reads=[gsqb, k.gCON["c_ind2"]], writes=[k.gPS[6]])
                    k.op("vector", lambda e, ts=ts: e.reduce_max(
                            qm[:, ts:ts + 1], k.PS[0:2, 6, :], axis=AX.X),
                         reads=[k.gPS[6]],
                         writes=[gqm] if ts == 0 else [], rw=[] if ts == 0 else [gqm])
                k.op("vector", lambda e: e.tensor_mul(negB, qm, kr),
                     reads=[gqm, gkr], writes=[gnegB])
                _c38 = self.ccol(1e-38)
                _c05 = self.ccol(0.5)
                k.op("scalar", lambda e, _c=_c38: e.activation(negB, negB, AF.Ln,
                                                      bias=_c[0:2]),
                     reads=[gcc], rw=[gnegB])
                k.op("scalar", lambda e, _c=_c05: e.activation(negB, negB, AF.Exp,
                                                      scale=_c[0:2]),
                     reads=[gcc], rw=[gnegB])
                k.op("vector", lambda e: e.tensor_scalar_mul(negB, negB, -float(scale)),
                     rw=[gnegB])
                k.op("tensor", lambda e: e.transpose(k.PS[0:8, 6, 0:2], negB,
                                                     identf[0:2, 0:2]),
                     reads=[gnegB, k.gCON["c_identf"]], writes=[k.gPS[6]])
                k.op("vector", lambda e: e.tensor_reduce(negT, k.PS[0:8, 6, 0:2], AX.X,
                                                         ALU.min),
                     reads=[k.gPS[6]], writes=[gnegT])
                k.op("tensor", lambda e: e.transpose(k.PS[0:1, 6, 0:8], negT,
                                                     identf[0:8, 0:8]),
                     reads=[gnegT, k.gCON["c_identf"]], writes=[k.gPS[6]])
                k.op("scalar", lambda e: e.copy(brow, k.PS[0:1, 6, 0:8]),
                     reads=[k.gPS[6]], writes=[gbrow])
                k.op("tensor", lambda e: e.matmul(
                        k.PS[:, 6, 0:8], k.CON["c_onesf"][0:1, :], brow[0:1, 0:8],
                        start=True, stop=True, tile_position=(0, 0)),
                     reads=[gbrow, k.gCON["c_onesf"]], writes=[k.gPS[6]])
                k.op("scalar", lambda e: e.copy(biasc8, k.PS[:, 6, 0:8]),
                     reads=[k.gPS[6]], writes=[gbias])
                for qg in range(8):
                    nkt = 4 * qg + 4
                    avb = 4
                    def scores_g(g, qg=qg, pair=pair):
                        for h, tp in ((0, (0, 0)), (1, (64, 0))):
                            bnk = 2 * (g % 2) + h
                            k.op("tensor", lambda e, g=g, h=h, tp=tp, bnk=bnk, qg=qg, pair=pair: e.matmul(
                                    k.PS[:, bnk, :],
                                    kdup[h * 64:h * 64 + 64, g * 128:(g + 1) * 128],
                                    Bv[h * 64:h * 64 + 64, pair, qg * 512:(qg + 1) * 512],
                                    start=True, stop=True, tile_position=tp),
                                 reads=[gkd, self.gBcs(pair, qg)], writes=[k.gPS[bnk]])
                        k.op("scalar", lambda e, g=g, qg=qg: e.activation(
                                attnT[:, :, g % 2, :], k.PS[:, 2 * (g % 2):2 * (g % 2) + 2, :],
                                AF.Exp, bias=biasc8[:, qg:qg + 1], scale=cscale),
                             reads=[k.gPS[2 * (g % 2)], k.gPS[2 * (g % 2) + 1], gbias, gcc],
                             writes=[gaT[0][g % 2], gaT[1][g % 2]])
                        o = g - 4 * qg
                        if 0 <= o <= 3:
                            for h in (0, 1):
                                k.op("vector", lambda e, g=g, h=h, o=o: e.tensor_mul(
                                        attnT[:, h, g % 2, :], attnT[:, h, g % 2, :],
                                        masks[:, o, :]),
                                     reads=[k.gCON["c_masks"]], rw=[gaT[h][g % 2]])

                    def av_g(g, nkt=nkt, avb=avb):
                        for h in (0, 1):
                            k.op("tensor", lambda e, g=g, h=h, nkt=nkt, avb=avb: e.matmul(
                                    k.PS[0:65, avb + h, :], v[:, g, :], attnT[:, h, g % 2, :],
                                    start=(g == 0), stop=(g == nkt - 1)),
                                 reads=[gv, gaT[h][g % 2]],
                                 writes=[k.gPS[avb + h]] if g == 0 else [],
                                 rw=[] if g == 0 else [k.gPS[avb + h]])

                    scores_g(0)
                    for g in range(1, nkt):
                        scores_g(g)
                        av_g(g - 1)
                    av_g(nkt - 1)
                    for h in (0, 1):
                        k.op("scalar", lambda e, h=h, avb=avb: e.activation(
                                rlrow, k.PS[64:65, avb + h, :], AF.Ln),
                             reads=[k.gPS[avb + h]], writes=[grl])
                        k.op("scalar", lambda e: e.activation(
                                rlbrow, rlrow, AF.Exp, scale=self.ccol(-1.0)[0:1]),
                             reads=[grl, gcc], writes=[grlrow])
                        k.op("tensor", lambda e: e.matmul(
                                k.PS[0:64, 6, :], ones[0:1, 0:64], rlbrow,
                                start=True, stop=True, tile_position=(0, 0)),
                             reads=[grlrow, k.gCON["c_ones"]], writes=[k.gPS[6]])
                        k.op("scalar", lambda e: e.copy(rlb, k.PS[0:64, 6, :]),
                             reads=[k.gPS[6]], writes=[grlb])
                        if h == 0:
                            k.op("vector", lambda e, pair=pair, qg=qg, avb=avb: e.tensor_mul(
                                    Bv[0:64, pair, qg * 512:(qg + 1) * 512],
                                    k.PS[0:64, avb, :], rlb),
                                 reads=[k.gPS[avb], grlb], rw=[self.gBcs(pair, qg)])
                        else:
                            k.op("vector", lambda e, qg=qg, avb=avb: e.tensor_mul(
                                    stage[:, qg * 512:(qg + 1) * 512], k.PS[0:64, avb + 1, :], rlb),
                                 reads=[k.gPS[avb + 1], grlb], rw=[gsq])
                k.op("gpsimd", lambda e, pair=pair: e.dma_start(
                        out=Bv[64:128, pair, :], in_=stage),
                     reads=[gsq], writes=[[self.gBcs(pair, t) for t in range(8)]], dma=True)

    def ph_t2(self):
        k = self.k
        g2v, gg2 = self.wv("l1f_g2")
        sqb = self.Ub(0, 512)
        gsq = G()
        rstdb = self.Uf(512, 512)
        grstdb = G()
        for s in range(8):
            self.stats_slice(lambda dc, s=s: (k.A[:, dc, s * 512:(s + 1) * 512],
                                              k.gA[dc][s]), sqb, gsq, do_sum=False)
            rr, grr = self.row(0), k.gR[0]
            k.op("scalar", lambda e, rr=rr: e.activation(
                    rr, k.PS[0:1, 7, :], AF.Identity, scale=self.ccol(1.0 / 1024.0)[0:1]),
                 reads=[k.gPS[7], k.gR[7]], writes=[grr])
            k.op("scalar", lambda e, rr=rr: e.activation(
                    rr, rr, AF.Ln, bias=self.ccol(1.1920928955078125e-07)[0:1]),
                 reads=[k.gR[7]], rw=[grr])
            k.op("scalar", lambda e, rr=rr: e.activation(
                    rr, rr, AF.Exp, scale=self.ccol(-0.5)[0:1]),
                 reads=[k.gR[7]], rw=[grr])
            self.bcast_f32(rr, grr, rstdb, grstdb)
            for dc in range(8):
                ap = k.A[:, dc, s * 512:(s + 1) * 512]
                k.op("vector", lambda e, ap=ap: e.tensor_mul(ap, ap, rstdb),
                     reads=[grstdb], rw=[k.gA[dc][s]])
                k.op("vector", lambda e, ap=ap, dc=dc: e.tensor_scalar(
                        ap, ap, g2v[:, dc:dc + 1], None, ALU.mult),
                     reads=[gg2], rw=[k.gA[dc][s]])

    def ph_epilogue(self, blk):
        k = self.k
        xs, xb = k.dram["xs"], k.dram["xb"]
        final = blk == "l1f"
        weighted = blk == "l0f"
        xsl = self.Uf(0, 4096).rearrange("p (c t) -> p c t", c=8)
        gxsl = G()
        identf = k.CON["c_identf"]
        if blk in ("l0a", "l1a"):
            xbl = self.Ub(8192, 4096).rearrange("p (c t) -> p c t", c=8)
            sqb = self.Ub(12288, 512)
            scl2 = self.Uf(12800, 512)
            sft2 = self.Ub(13824, 512)
            scl3 = self.Uf(14336, 512)
            sft3 = self.Ub(15360, 512)
        elif blk == "l0f":
            xbl = self.Ub(8192, 4096).rearrange("p (c t) -> p c t", c=8)
            mt = self.Ub(12288, 4096).rearrange("p (c t) -> p c t", c=8)
            gcols = self.Ub(16384, 48).rearrange("p (c f) -> p c f", c=8)
            sqb = self.Ub(16448, 512)
            scl2 = self.Uf(16960, 512)
            sft2 = self.Ub(17984, 512)
            rfb = self.Uf(18496, 512)
            rf2b = self.Ub(19520, 512)
            scl3, sft3 = scl2, sft2
        else:
            stagef = self.Uf(8192, 1024)
            mt = self.Ub(10240, 4096).rearrange("p (c t) -> p c t", c=8)
            sqb = self.Ub(14336, 512)
            scl2 = self.Uf(14848, 512)
            sft2 = self.Ub(15872, 512)
        gxbl, gmt, gsq = G(), G(), G()
        gscl2, gsft2, grfb, grf2b = G(), G(), G(), G()
        if weighted:
            gscl3, gsft3 = gscl2, gsft2
        else:
            gscl3, gsft3 = G(), G()
        gstagef = G()
        if weighted:
            gv_, gg = self.wv("l0f_g")
            bv_, gb = self.wv("l0f_b")
            ggcols = G()
            k.op("vector", lambda e: e.memset(gcols[:, :, 0:1], 1.0), writes=[ggcols])
            for dc in range(8):
                k.op("vector", lambda e, dc=dc: e.tensor_copy(
                        gcols[:, dc, 1:2], gv_[:, dc:dc + 1]), reads=[gg], rw=[ggcols])
                k.op("vector", lambda e, dc=dc: e.tensor_mul(
                        gcols[:, dc, 2:3], gv_[:, dc:dc + 1], gv_[:, dc:dc + 1]),
                     reads=[gg], rw=[ggcols])
                k.op("vector", lambda e, dc=dc: e.tensor_mul(
                        gcols[:, dc, 3:4], gv_[:, dc:dc + 1], bv_[:, dc:dc + 1]),
                     reads=[gg, gb], rw=[ggcols])
                k.op("vector", lambda e, dc=dc: e.tensor_copy(
                        gcols[:, dc, 4:5], bv_[:, dc:dc + 1]), reads=[gb], rw=[ggcols])
                k.op("vector", lambda e, dc=dc: e.tensor_mul(
                        gcols[:, dc, 5:6], bv_[:, dc:dc + 1], bv_[:, dc:dc + 1]),
                     reads=[gb], rw=[ggcols])
            scal = self.row(6)
            gscal = k.gR[6]
            for i, col in enumerate((1, 2, 3, 4, 5)):
                for dc in range(8):
                    k.op("tensor", lambda e, dc=dc, col=col: e.matmul(
                            k.PS[0:1, 5, 0:1], k.CON["c_ones"][:, 0:1],
                            gcols[:, dc, col:col + 1],
                            start=(dc == 0), stop=(dc == 7), tile_position=(0, 0)),
                         reads=[ggcols, k.gCON["c_ones"]],
                         writes=[k.gPS[5]] if dc == 0 else [],
                         rw=[] if dc == 0 else [k.gPS[5]])
                k.op("scalar", lambda e, i=i: e.copy(scal[0:1, i:i + 1], k.PS[0:1, 5, 0:1]),
                     reads=[k.gPS[5]],
                     writes=[gscal] if i == 0 else [], rw=[] if i == 0 else [gscal])

        nxt = {"l0a": ("ln2", 1e-5, True), "l0f": ("ln", None, False),
               "l1a": ("ln", None, False), "l1f": (None, None, False)}[blk]
        if blk == "l0a":
            gaffg, ggaffg = self.wv("l0f_g")
            gaffb, ggaffb = self.wv("l0f_b")

        for s in range(8):
            k.op("sync", lambda e, s=s: e.dma_start(
                    out=xsl, in_=xs.rearrange("c p t -> p c t")[:, :, s * 512:(s + 1) * 512]),
                 reads=[k.gXS[s]], writes=[gxsl], dma=True)
            if weighted:
                k.op("sync", lambda e, s=s: e.dma_start(
                        out=mt, in_=k.dram["moeb"].rearrange(
                            "c p t -> p c t")[:, :, s * 512:(s + 1) * 512]),
                     reads=[k.gMOEB], writes=[gmt], dma=True)
                bout = lambda dc: (mt[:, dc, :], gmt)
            elif final:
                k.op("sync", lambda e, s=s: e.dma_start(
                        out=mt, in_=k.dram["moeb"].rearrange(
                            "c p t -> p c t")[:, :, s * 512:(s + 1) * 512]),
                     reads=[k.gMOEB], writes=[gmt], dma=True)
                k.op("gpsimd", lambda e, s=s: e.dma_start(
                        out=mt, in_=k.dram["ub"].rearrange(
                            "c p t -> p c t")[:, :, s * 512:(s + 1) * 512],
                        accum_op=ALU.add),
                     reads=[k.gUB], rw=[gmt], dma=True)
                bout = lambda dc: (mt[:, dc, :], gmt)
            else:
                bout = lambda dc, s=s: (k.A[:, dc, s * 512:(s + 1) * 512], k.gA[dc][s])

            if not weighted:
                self.stats_slice(lambda dc, bout=bout: bout(dc), sqb, gsq)
                self.ln_rows("ln")
                self.bcast_f32(self.row(0), k.gR[0], scl2, gscl2)
                self.bcast_f32(self.row(1), k.gR[1], sft2, gsft2)
            else:
                self.weighted_rows(bout, sqb, gsq, gcols, ggcols, scal, gscal,
                                   [(scl2, gscl2), (sft2, gsft2), (rfb, grfb),
                                    (rf2b, grf2b)])

            for dc in range(8):
                bap, gba = bout(dc)
                xv = xsl[:, dc, :]
                k.op("vector", lambda e, bap=bap: e.tensor_mul(sqb, bap, scl2),
                     reads=[gba, gscl2], writes=[gsq])
                k.op("vector", lambda e: e.tensor_add(sqb, sqb, sft2),
                     reads=[gsft2], rw=[gsq])
                if weighted:
                    k.op("vector", lambda e, dc=dc: e.tensor_scalar(
                            sqb, sqb, gv_[:, dc:dc + 1], None, ALU.mult),
                         reads=[gg], rw=[gsq])
                k.op("vector", lambda e, xv=xv: e.tensor_add(xv, xv, sqb),
                     reads=[gsq], rw=[gxsl])
                if weighted:
                    k.op("vector", lambda e, dc=dc: e.tensor_scalar(
                            sqb, rfb, bv_[:, dc:dc + 1], None, ALU.mult),
                         reads=[grfb, gb], writes=[gsq])
                    k.op("vector", lambda e: e.tensor_add(sqb, sqb, rf2b),
                         reads=[grf2b], rw=[gsq])
                    k.op("vector", lambda e, xv=xv: e.tensor_add(xv, xv, sqb),
                         reads=[gsq], rw=[gxsl])
                if not final:
                    k.op("vector", lambda e, dc=dc, xv=xv: e.tensor_copy(xbl[:, dc, :], xv),
                         reads=[gxsl],
                         writes=[gxbl] if dc == 0 else [], rw=[] if dc == 0 else [gxbl])
            if final:
                for tq in range(4):
                    for dc in range(8):
                        pb = dc % 2
                        k.op("tensor", lambda e, tq=tq, dc=dc, pb=pb: e.transpose(
                                k.PS[:, pb, 0:128], xsl[:, dc, tq * 128:(tq + 1) * 128],
                                identf[:]),
                             reads=[gxsl, k.gCON["c_identf"]], writes=[k.gPS[pb]])
                        k.op("scalar", lambda e, dc=dc, pb=pb: e.copy(
                                stagef[:, dc * 128:(dc + 1) * 128], k.PS[:, pb, 0:128]),
                             reads=[k.gPS[pb]],
                             writes=[gstagef] if dc == 0 else [],
                             rw=[] if dc == 0 else [gstagef])
                    tr = s * 4 + tq
                    k.op("sync", lambda e, tr=tr: e.dma_start(
                            out=k.dram["outy"][tr * 128:(tr + 1) * 128, :], in_=stagef),
                         reads=[gstagef], writes=[k.gOUT[s]], dma=True)
                continue
            k.op("sync", lambda e, s=s: e.dma_start(
                    out=xs.rearrange("c p t -> p c t")[:, :, s * 512:(s + 1) * 512], in_=xsl),
                 reads=[gxsl], writes=[k.gXS[s]], dma=True)
            k.op("sync", lambda e, s=s: e.dma_start(
                    out=xb.rearrange("c p t -> p c t")[:, :, s * 512:(s + 1) * 512], in_=xbl),
                 reads=[gxbl], writes=[k.gXB[s]], dma=True)
            mode, eps2, aff = nxt
            self.stats_slice(lambda dc: (xbl[:, dc, :], gxbl), sqb, gsq)
            self.ln_rows(mode, eps2=eps2)
            self.bcast_f32(self.row(0), k.gR[0], scl3, gscl3)
            self.bcast_f32(self.row(1), k.gR[1], sft3, gsft3)
            for dc in range(8):
                ap = k.A[:, dc, s * 512:(s + 1) * 512]
                k.op("vector", lambda e, ap=ap, dc=dc: e.tensor_mul(ap, xbl[:, dc, :], scl3),
                     reads=[gxbl, gscl3], writes=[k.gA[dc][s]])
                k.op("vector", lambda e, ap=ap: e.tensor_add(ap, ap, sft3),
                     reads=[gsft3], rw=[k.gA[dc][s]])
                if aff:
                    k.op("vector", lambda e, ap=ap, dc=dc: e.tensor_scalar(
                            ap, ap, gaffg[:, dc:dc + 1], gaffb[:, dc:dc + 1],
                            ALU.mult, ALU.add),
                         reads=[ggaffg, ggaffb], rw=[k.gA[dc][s]])

    def weighted_rows(self, bout, sqb, gsq, gcols, ggcols, scal, gscal, bcs):
        (bc0, gbc0), (bc1, gbc1), (bc2, gbc2), (bc3, gbc3) = bcs
        k = self.k
        for dc in range(8):
            bap, gba = bout(dc)
            for bk, col in ((0, 0), (1, 1), (2, 2), (3, 3)):
                k.op("tensor", lambda e, dc=dc, bk=bk, col=col, bap=bap: e.matmul(
                        k.PS[0:1, bk, :], gcols[:, dc, col:col + 1], bap,
                        start=(dc == 0), stop=(dc == 7), tile_position=(0, 0)),
                     reads=[gba, ggcols],
                     writes=[k.gPS[bk]] if dc == 0 else [], rw=[] if dc == 0 else [k.gPS[bk]])
            k.op("scalar", lambda e, bap=bap: e.activation(sqb, bap, AF.Square),
                 reads=[gba], writes=[gsq])
            for bk, col in ((6, 0), (7, 2)):
                k.op("tensor", lambda e, dc=dc, bk=bk, col=col: e.matmul(
                        k.PS[0:1, bk, :], gcols[:, dc, col:col + 1], sqb,
                        start=(dc == 0), stop=(dc == 7), tile_position=(0, 0)),
                     reads=[gsq, ggcols],
                     writes=[k.gPS[bk]] if dc == 0 else [], rw=[] if dc == 0 else [k.gPS[bk]])
        r = {i: (self.row(i), k.gR[i]) for i in range(6)}
        c1024 = self.ccol(1.0 / 1024.0)
        gcc = k.gR[7]
        mu, gmu = r[0]
        s_, gs = r[1]
        Smg, gSmg = r[2]
        t0, gt0 = r[3]
        t1, gt1 = r[4]
        t2, gt2 = r[5]
        k.op("scalar", lambda e: e.activation(mu, k.PS[0:1, 0, :], AF.Identity, scale=c1024[0:1]),
             reads=[k.gPS[0], gcc], writes=[gmu])
        k.op("scalar", lambda e: e.activation(t0, k.PS[0:1, 6, :], AF.Identity, scale=c1024[0:1]),
             reads=[k.gPS[6], gcc], writes=[gt0])
        k.op("vector", lambda e: e.tensor_mul(s_, mu, mu), reads=[gmu], writes=[gs])
        k.op("vector", lambda e: e.tensor_sub(s_, t0, s_), reads=[gt0], rw=[gs])
        _c = self.ccol(1e-5)
        k.op("scalar", lambda e, _c=_c: e.activation(s_, s_, AF.Ln, bias=_c[0:1]),
             reads=[gcc], rw=[gs])
        _c = self.ccol(-0.5)
        k.op("scalar", lambda e, _c=_c: e.activation(s_, s_, AF.Exp, scale=_c[0:1]),
             reads=[gcc], rw=[gs])
        k.op("scalar", lambda e: e.copy(Smg, k.PS[0:1, 1, :]), reads=[k.gPS[1]], writes=[gSmg])
        G1 = scal[0:1, 0:1]
        G2 = scal[0:1, 1:2]
        GB = scal[0:1, 2:3]
        B1 = scal[0:1, 3:4]
        B2 = scal[0:1, 4:5]
        k.op("vector", lambda e: e.tensor_scalar(t0, mu, G1, None, ALU.mult),
             reads=[gmu, gscal], writes=[gt0])
        k.op("vector", lambda e: e.tensor_sub(t0, Smg, t0), reads=[gSmg], rw=[gt0])
        k.op("vector", lambda e: e.tensor_mul(t0, t0, s_), reads=[gs], rw=[gt0])
        k.op("vector", lambda e: e.tensor_scalar(t0, t0, B1, None, ALU.add),
             reads=[gscal], rw=[gt0])
        k.op("vector", lambda e: e.tensor_scalar_mul(t0, t0, 1.0 / 1024.0), rw=[gt0])
        k.op("scalar", lambda e: e.copy(t1, k.PS[0:1, 7, :]), reads=[k.gPS[7]], writes=[gt1])
        k.op("scalar", lambda e: e.copy(t2, k.PS[0:1, 2, :]), reads=[k.gPS[2]], writes=[gt2])
        k.op("vector", lambda e: e.tensor_mul(t2, t2, mu), reads=[gmu], rw=[gt2])
        k.op("vector", lambda e: e.tensor_scalar_mul(t2, t2, -2.0), rw=[gt2])
        k.op("vector", lambda e: e.tensor_add(t1, t1, t2), reads=[gt2], rw=[gt1])
        k.op("vector", lambda e: e.tensor_mul(t2, mu, mu), reads=[gmu], writes=[gt2])
        k.op("vector", lambda e: e.tensor_scalar(t2, t2, G2, None, ALU.mult),
             reads=[gscal], rw=[gt2])
        k.op("vector", lambda e: e.tensor_add(t1, t1, t2), reads=[gt2], rw=[gt1])
        k.op("vector", lambda e: e.tensor_mul(t1, t1, s_), reads=[gs], rw=[gt1])
        k.op("vector", lambda e: e.tensor_mul(t1, t1, s_), reads=[gs], rw=[gt1])
        k.op("scalar", lambda e: e.copy(t2, k.PS[0:1, 3, :]), reads=[k.gPS[3]], writes=[gt2])
        k.op("vector", lambda e: e.tensor_scalar(Smg, mu, GB, None, ALU.mult),
             reads=[gmu, gscal], writes=[gSmg])
        k.op("vector", lambda e: e.tensor_sub(t2, t2, Smg), reads=[gSmg], rw=[gt2])
        k.op("vector", lambda e: e.tensor_mul(t2, t2, s_), reads=[gs], rw=[gt2])
        k.op("vector", lambda e: e.tensor_scalar_mul(t2, t2, 2.0), rw=[gt2])
        k.op("vector", lambda e: e.tensor_add(t1, t1, t2), reads=[gt2], rw=[gt1])
        k.op("vector", lambda e: e.tensor_scalar(t1, t1, B2, None, ALU.add),
             reads=[gscal], rw=[gt1])
        k.op("vector", lambda e: e.tensor_scalar_mul(t1, t1, 1.0 / 1024.0), rw=[gt1])
        k.op("vector", lambda e: e.tensor_mul(t2, t0, t0), reads=[gt0], writes=[gt2])
        k.op("vector", lambda e: e.tensor_sub(t1, t1, t2), reads=[gt2], rw=[gt1])
        _c = self.ccol(1e-5)
        k.op("scalar", lambda e, _c=_c: e.activation(t1, t1, AF.Ln, bias=_c[0:1]),
             reads=[gcc], rw=[gt1])
        _c = self.ccol(-0.5)
        k.op("scalar", lambda e, _c=_c: e.activation(t1, t1, AF.Exp, scale=_c[0:1]),
             reads=[gcc], rw=[gt1])
        k.op("vector", lambda e: e.tensor_mul(t2, s_, t1), reads=[gs, gt1], writes=[gt2])
        self.bcast_f32(t2, gt2, bc0, gbc0)
        k.op("vector", lambda e: e.tensor_mul(t2, t2, mu), reads=[gmu], rw=[gt2])
        k.op("vector", lambda e: e.tensor_scalar_mul(t2, t2, -1.0), rw=[gt2])
        self.bcast_f32(t2, gt2, bc1, gbc1)
        self.bcast_f32(t1, gt1, bc2, gbc2)
        k.op("vector", lambda e: e.tensor_mul(t0, t0, t1), reads=[gt1], rw=[gt0])
        k.op("vector", lambda e: e.tensor_scalar_mul(t0, t0, -1.0), rw=[gt0])
        self.bcast_f32(t0, gt0, bc3, gbc3)


def kernel(**inputs):
    x = np.asarray(inputs["L0_attn_x"])
    prepped = prep_inputs(inputs)
    b = Builder(("all",), sim_safe={"vector"})
    nc = b.build()
    in_maps = []
    for c in range(8):
        m = dict(prepped)
        m["x0"] = np.ascontiguousarray(x[c])
        in_maps.append(m)
    res = run_bass_kernel_spmd(nc, in_maps, core_ids=list(range(8)))
    return np.stack([res.results[c]["outy"] for c in range(8)], axis=0)
